# version 02
# baseline (speedup 1.0000x reference)
import numpy as np
import ml_dtypes
import concourse.bass as bass
import concourse.mybir as mybir
from concourse.bass import IndirectOffsetOnAxis
from concourse.bass_utils import run_bass_kernel_spmd
from contextlib import ExitStack

F32 = mybir.dt.float32
BF16 = mybir.dt.bfloat16
U32 = mybir.dt.uint32
AF = mybir.ActivationFunctionType
ALU = mybir.AluOpType
AX = mybir.AxisListType
MAGIC = float(1.5 * 2 ** 23)
QMAX = 32767.0
LN_EPS = 1e-5
RMS_EPS = 1e-6
BF = ml_dtypes.bfloat16
KSTAGE = 4
LAST_SEMS = {}


class KB:
    def __init__(self, nc, blk, stk):
        self.nc, self.blk, self.stk = nc, blk, stk
        self.sem = {}
        for name in ["qS", "qC", "qV", "qT", "qG", "sT", "sV", "sA", "sG", "sCC"]:
            self.sem[name] = [stk.enter_context(nc.semaphore(name)), 0]

    def inc(self, instr, name, k=None):
        if k is None:
            k = 16 if name.startswith("q") else 1
        instr.then_inc(self.sem[name][0], k)
        self.sem[name][1] += k
        return self.sem[name][1]

    def n(self, name):
        return self.sem[name][1]

    def w(self, eng, name, val=None):
        eng.wait_ge(self.sem[name][0], self.n(name) if val is None else val)


def build(stage=4):
    nc = bass.Bass("TRN2", target_bir_lowering=False, debug=False,
                   detect_race_conditions=False)
    DT = nc.dram_tensor

    x_in = DT("x", [4096, 1024], F32, kind="ExternalInput").ap()
    a_wiT = DT("a_wiT", [1024, 3072], F32, kind="ExternalInput").ap()
    a_bin = DT("a_bin", [128, 24], F32, kind="ExternalInput").ap()
    a_woT = DT("a_woT", [1024, 1024], F32, kind="ExternalInput").ap()
    a_bout = DT("a_bout", [128, 8], F32, kind="ExternalInput").ap()
    f_wu = DT("f_wu", [3072, 1024], F32, kind="ExternalInput").ap()
    f_wg = DT("f_wg", [3072, 1024], F32, kind="ExternalInput").ap()
    f_wd = DT("f_wd", [1024, 3072], F32, kind="ExternalInput").ap()
    s_A = DT("s_A", [128, 4, 16], F32, kind="ExternalInput").ap()
    s_wdT = DT("s_wdT", [1024, 512], F32, kind="ExternalInput").ap()
    s_bd = DT("s_bd", [128, 4], F32, kind="ExternalInput").ap()
    s_wbT = DT("s_wbT", [1024, 16], F32, kind="ExternalInput").ap()
    s_bb = DT("s_bb", [16, 1], F32, kind="ExternalInput").ap()
    s_wcT = DT("s_wcT", [1024, 16], F32, kind="ExternalInput").ap()
    s_bc = DT("s_bc", [16, 1], F32, kind="ExternalInput").ap()
    m_gwT = DT("m_gwT", [1024, 64], F32, kind="ExternalInput").ap()
    m_w13 = DT("m_w13", [8192, 1024], F32, kind="ExternalInput").ap()
    m_w2 = DT("m_w2", [4096, 1024], F32, kind="ExternalInput").ap()
    m_swg = DT("m_swg", [1024, 512], F32, kind="ExternalInput").ap()
    m_swu = DT("m_swu", [1024, 512], F32, kind="ExternalInput").ap()
    m_swd = DT("m_swd", [512, 1024], F32, kind="ExternalInput").ap()
    m_rmsw = DT("m_rmsw", [128, 1024], F32, kind="ExternalInput").ap()
    c_eye = DT("c_eye", [128, 128], F32, kind="ExternalInput").ap()
    c_eyeb = DT("c_eyeb", [128, 128], BF16, kind="ExternalInput").ap()
    c_striu = DT("c_striu", [128, 128], BF16, kind="ExternalInput").ap()
    c_mask = DT("c_mask", [128, 128], F32, kind="ExternalInput").ap()
    c_exp32 = DT("c_exp32", [32, 1024], F32, kind="ExternalInput").ap()
    c_exp96 = DT("c_exp96", [96, 3072], F32, kind="ExternalInput").ap()
    c_eb = DT("c_eb", [16, 2048], BF16, kind="ExternalInput").ap()
    c_onesr = DT("c_onesr", [1, 128], F32, kind="ExternalInput").ap()
    c_onescb = DT("c_onescb", [128, 1], BF16, kind="ExternalInput").ap()
    c_iota64 = DT("c_iota64", [128, 64], F32, kind="ExternalInput").ap()
    c_mprev = DT("c_mprev", [8, 1], F32, kind="ExternalInput").ap()
    c_tg1 = DT("c_tg1", [128, 1], F32, kind="ExternalInput").ap()
    c_xtbase = DT("c_xtbase", [128, 1], F32, kind="ExternalInput").ap()
    c_ytbase = DT("c_ytbase", [128, 1], F32, kind="ExternalInput").ap()
    c_p128 = DT("c_p128", [128, 1], F32, kind="ExternalInput").ap()

    out_h = DT("out", [4096, 1024], F32, kind="ExternalOutput").ap()
    dbg_h = DT("dbg", [128, 16], F32, kind="ExternalOutput").ap()

    xr = DT("xr", [4096, 1024], F32).ap()
    wiT_bf = DT("wiT_bf", [1024, 3072], BF16).ap()
    woT_bf = DT("woT_bf", [1024, 1024], BF16).ap()
    wupT_q = DT("wupT_q", [1024, 3072], BF16).ap()
    wgT_q = DT("wgT_q", [1024, 3072], BF16).ap()
    wdT_q = DT("wdT_q", [3072, 1024], BF16).ap()
    st_up = DT("st_up", [3072, 4096], F32).ap()
    st_gate = DT("st_gate", [3072, 4096], F32).ap()
    st_g2 = DT("st_g2", [3072, 4096], F32).ap()
    st_o = DT("st_o", [3072, 4096], F32).ap()
    wdTh_bf = DT("wdTh_bf", [1024, 512], BF16).ap()
    wbT_bf = DT("wbT_bf", [1024, 16], BF16).ap()
    wcT_bf = DT("wcT_bf", [1024, 16], BF16).ap()
    w13_bf = DT("w13_bf", [8192, 1024], BF16).ap()
    w2_bf = DT("w2_bf", [4096, 1024], BF16).ap()
    swg_bf = DT("swg_bf", [1024, 512], BF16).ap()
    swu_bf = DT("swu_bf", [1024, 512], BF16).ap()
    swd_bf = DT("swd_bf", [512, 1024], BF16).ap()
    xT_loc = DT("xT_loc", [1024, 4096], BF16).ap()
    mark = DT("mark", [1, 4096], BF16).ap()
    xT_all = DT("xT_all", [2048, 4096], BF16).ap()
    brep_all = DT("brep_all", [16, 128, 8192], BF16).ap()
    crep_all = DT("crep_all", [16, 128, 8192], BF16).ap()
    ytm_loc = DT("ytm_loc", [8192, 512], F32).ap()
    ytm_all = DT("ytm_all", [16384, 512], F32).ap()
    am1_i = DT("am1_i", [1, 1], F32).ap()
    am1_o = DT("am1_o", [1, 1], F32).ap()
    am2_i = DT("am2_i", [1, 2], F32).ap()
    am2_o = DT("am2_o", [1, 2], F32).ap()
    am3_i = DT("am3_i", [1, 1], F32).ap()
    am3_o = DT("am3_o", [1, 1], F32).ap()
    am4_i = DT("am4_i", [1, 1], F32).ap()
    am4_o = DT("am4_o", [1, 1], F32).ap()
    cnt_i = DT("cnt_i", [1, 64], F32).ap()
    cnt_o = DT("cnt_o", [8, 64], F32).ap()
    tok_i = DT("tok_i", [65600, 4], F32).ap()
    tok_my = DT("tok_my", [8200, 4], F32).ap()

    xn_loc = DT("xn_loc", [4096, 1024], BF16).ap()
    X_all = DT("X_all", [32768, 1024], BF16).ap()
    Y_own = DT("Y_own", [8192, 1024], BF16).ap()
    Y_all = DT("Y_all", [65536, 1024], BF16).ap()

    stk = ExitStack()
    with stk:
        sb = lambda n, s, dt=F32: stk.enter_context(nc.sbuf_tensor(n, s, dt))
        eye = sb("eye", [128, 128])
        eyeb = sb("eyeb", [128, 128], BF16)
        striu = sb("striu", [128, 128], BF16)
        maskc = sb("maskc", [128, 128])
        iota64 = sb("iota64", [128, 64])
        onesr = sb("onesr", [1, 128])
        onescb = sb("onescb", [128, 1], BF16)
        mprev = sb("mprev", [8, 1])
        tg1 = sb("tg1", [128, 1])
        xtbase = sb("xtbase", [128, 1])
        p128 = sb("p128", [128, 1])
        ytbase = sb("ytbase", [128, 1])
        A_sb = sb("A_sb", [128, 4, 16])
        bd_sb = sb("bd_sb", [128, 4])
        rmsw_sb = sb("rmsw_sb", [128, 1024])
        negm_s = sb("negm_s", [128, 32])
        rstd_s = sb("rstd_s", [128, 32])
        rmsr_s = sb("rmsr_s", [128, 32])
        bns = sb("bns", [128, 12])
        st2 = sb("st2", [128, 2])
        sq1 = sb("sq1", [128, 1])
        scal = sb("scal", [128, 8])
        scal2 = sb("scal2", [128, 4])
        dbg = sb("dbg_sb", [128, 16])
        ceps = sb("ceps", [128, 2])
        trow = sb("trow", [1, 16])
        trow2 = sb("trow2", [1, 128])
        pb = [stk.enter_context(nc.psum_tensor(f"pb{i}", [128, 512], F32)) for i in range(7)]
        pbb = stk.enter_context(nc.psum_tensor("pbb", [128, 1024], BF16))
        blk = stk.enter_context(nc.Block())
        K = KB(nc, blk, stk)

        def ln_chain(src_ap, dst_ap, col, eps=LN_EPS, qwaits=(), pre_v=None, rms_col=None):
            @blk.vector
            def _(v):
                for nm, val in qwaits:
                    K.w(v, nm, val)
                if pre_v is not None:
                    pre_v(v)
                v.bn_stats(bns[:, 0:6], src_ap[:, 0:512])
                v.bn_stats(bns[:, 6:12], src_ap[:, 512:1024])
                v.drain()
                v.bn_aggr(st2[:], bns[:])
                v.drain()
                K.inc(v.tensor_scalar(negm_s[:, col:col + 1], st2[:, 0:1], float(-1.0),
                                      None, ALU.mult, ALU.bypass), "sV")

            @blk.scalar
            def _(a):
                K.w(a, "sV")
                K.w(a, "sG", nG_eps)
                eps_ap = ceps[:, 0:1] if eps == LN_EPS else ceps[:, 1:2]
                K.inc(a.activation(out=sq1[:], in_=st2[:, 1:2], func=AF.Sqrt,
                                   bias=eps_ap, scale=1.0), "sA")

            @blk.vector
            def _(v):
                K.w(v, "sA")
                v.reciprocal(rstd_s[:, col:col + 1], sq1[:])
                v.drain()
                K.inc(v.tensor_scalar(dst_ap, src_ap, negm_s[:, col:col + 1],
                                      rstd_s[:, col:col + 1], ALU.add, ALU.mult), "sV")

            if rms_col is not None:
                @blk.vector
                def _(v):
                    v.bn_stats(bns[:, 0:6], dst_ap[:, 0:512])
                    v.bn_stats(bns[:, 6:12], dst_ap[:, 512:1024])
                    v.drain()
                    v.bn_aggr(st2[:], bns[:])
                    v.drain()
                    v.tensor_tensor(sq1[:], st2[:, 0:1], st2[:, 0:1], ALU.mult)
                    v.drain()
                    K.inc(v.tensor_tensor(sq1[:], sq1[:], st2[:, 1:2], ALU.add), "sV")

                @blk.scalar
                def _(a):
                    K.w(a, "sV")
                    K.inc(a.activation(out=sq1[:], in_=sq1[:], func=AF.Sqrt,
                                       bias=ceps[:, 1:2], scale=1.0), "sA")

                @blk.vector
                def _(v):
                    K.w(v, "sA")
                    K.inc(v.reciprocal(rmsr_s[:, rms_col:rms_col + 1], sq1[:]), "sV")

        def bcast_scalar(src11_ap, dst_col_ap, recip_col_ap=None):
            @blk.tensor
            def _(t):
                K.w(t, "sV")
                K.inc(t.matmul(pb[6][:, 0:1], onesr[:, :], src11_ap, start=True,
                               stop=True), "sT")

            @blk.vector
            def _(v):
                K.w(v, "sT")
                if recip_col_ap is None:
                    K.inc(v.tensor_copy(dst_col_ap, pb[6][:, 0:1]), "sV")
                else:
                    v.tensor_copy(dst_col_ap, pb[6][:, 0:1])
                    v.drain()
                    K.inc(v.reciprocal(recip_col_ap, dst_col_ap), "sV")

        def global_absmax_multi(aps, ar_in, ar_out, dst_pairs):
            k = len(aps)
            for j, ap in enumerate(aps):
                @blk.vector
                def _(v, ap=ap):
                    K.inc(v.tensor_reduce(sq1[:], ap, axis=AX.X, op=ALU.max,
                                          apply_absolute_value=True), "sV")

                @blk.tensor
                def _(t):
                    K.w(t, "sV")
                    K.inc(t.transpose(pb[6][0:1, 0:128], sq1[:], eye[:]), "sT")

                @blk.vector
                def _(v, j=j):
                    K.w(v, "sT")
                    v.tensor_copy(trow2[0:1, :], pb[6][0:1, 0:128])
                    v.drain()
                    K.inc(v.tensor_reduce(trow[0:1, j:j + 1], trow2[0:1, :], axis=AX.X,
                                          op=ALU.max), "sV")

            @blk.gpsimd
            def _(g):
                K.w(g, "sV")
                K.inc(g.dma_start(out=ar_in, in_=trow[0:1, 0:k]), "qG")
                K.w(g, "qG")
                K.inc(g.collective_compute(
                    "AllReduce", ALU.max, replica_groups=[list(range(8))],
                    ins=[ar_in.opt()], outs=[ar_out.opt()]), "sCC")
                K.w(g, "sCC")
                K.inc(g.dma_start(out=trow[0:1, 4:4 + k], in_=ar_out), "qG")

            for j, (s_col, r_col) in enumerate(dst_pairs):
                @blk.vector
                def _(v, j=j):
                    K.w(v, "qG")
                    v.drain()
                    K.inc(v.tensor_scalar(trow[0:1, 8 + j:9 + j], trow[0:1, 4 + j:5 + j],
                                          float(1.0 / QMAX), float(1e-12), ALU.mult,
                                          ALU.max), "sV")
                bcast_scalar(trow[0:1, 8 + j:9 + j], s_col, r_col)

        def phase_barrier():
            @blk.sync
            def _(q):
                K.w(q, "qC"); K.w(q, "qG")

            @blk.scalar
            def _(a):
                K.w(a, "qS"); K.w(a, "qG")

            @blk.gpsimd
            def _(g):
                K.w(g, "qS"); K.w(g, "qC")

            @blk.vector
            def _(v):
                K.w(v, "qS"); K.w(v, "qC"); K.w(v, "qG")

            @blk.tensor
            def _(t):
                K.w(t, "qS"); K.w(t, "qC"); K.w(t, "qG")

        @blk.gpsimd
        def _(g):
            g.memset(ceps[:, 0:1], float(LN_EPS))
            K.inc(g.memset(ceps[:, 1:2], float(RMS_EPS)), "sG")
        nG_eps = K.n("sG")

        @blk.sync
        def _(q):
            for dst, src in [(eye, c_eye), (eyeb, c_eyeb), (striu, c_striu),
                             (maskc, c_mask), (iota64, c_iota64), (onesr, c_onesr),
                             (onescb, c_onescb), (mprev, c_mprev), (tg1, c_tg1),
                             (xtbase, c_xtbase), (ytbase, c_ytbase), (p128, c_p128),
                             (A_sb, s_A), (bd_sb, s_bd), (rmsw_sb, m_rmsw)]:
                K.inc(q.dma_start(out=dst[:], in_=src), "qS")
            for j in range(4):
                K.inc(q.dma_start(out=xr[j * 1024:(j + 1) * 1024, :],
                                  in_=x_in[j * 1024:(j + 1) * 1024, :]), "qS")
        nS_xr = K.n("qS")

        @blk.gpsimd
        def _(g):
            for dst, src in [(wiT_bf, a_wiT), (woT_bf, a_woT), (wdTh_bf, s_wdT),
                             (wbT_bf, s_wbT), (wcT_bf, s_wcT), (swg_bf, m_swg),
                             (swu_bf, m_swu), (swd_bf, m_swd)]:
                K.inc(g.dma_start(out=dst[:, :], in_=src[:, :]), "qG")
        nG_wiT = 16
        nG_woT = 32
        nG_smallcasts = K.n("qG")

        @blk.gpsimd
        def _(g):
            for j in range(8):
                K.inc(g.dma_start(out=w13_bf[j * 1024:(j + 1) * 1024, :],
                                  in_=m_w13[j * 1024:(j + 1) * 1024, :]), "qG")
            for j in range(4):
                K.inc(g.dma_start(out=w2_bf[j * 1024:(j + 1) * 1024, :],
                                  in_=m_w2[j * 1024:(j + 1) * 1024, :]), "qG")
        nG_casts = K.n("qG")

        def p1():
            with ExitStack() as s1:
                sb1 = lambda n, s, dt=F32: s1.enter_context(nc.sbuf_tensor(n, s, dt))
                xsel = sb1("p1_xsel", [128, 4, 1024])
                xln = sb1("p1_xln", [128, 4, 1024])
                xselT = sb1("p1_xselT", [128, 8, 512], BF16)
                wiT_sb = sb1("p1_wiT", [128, 8, 3072], BF16)
                bin_sb = sb1("p1_bin", [128, 24])
                woT_sb = sb1("p1_woT", [128, 8, 1024], BF16)
                bout_sb = sb1("p1_bout", [128, 8])
                qkvT = sb1("p1_qkvT", [128, 24, 512], BF16)
                pT = sb1("p1_pT", [128, 4, 512], BF16)
                sc = sb1("p1_sc", [128, 512])
                ptmp = sb1("p1_ptmp", [128, 512])
                pbf = sb1("p1_pbf", [128, 512], BF16)
                rr = sb1("p1_rr", [128, 4])
                vtm = sb1("p1_vtm", [128, 4, 64], BF16)
                attnT = sb1("p1_attnT", [128, 8, 512], BF16)
                aoutT = sb1("p1_aoutT", [128, 8, 512])
                aout = sb1("p1_aout", [128, 1024])
                lnout = sb1("p1_lnout", [128, 1024])
                upd = sb1("p1_upd", [128, 1024])

                @blk.gpsimd
                def _(g):
                    K.inc(g.memset(pT[:, :, :], 0.0), "sG")

                @blk.sync
                def _(q):
                    q.wait_ge(K.sem["qG"][0], nG_wiT)
                    K.inc(q.dma_start(
                        out=wiT_sb[:],
                        in_=wiT_bf.rearrange("(cc p) e -> p cc e", p=128)), "qS")
                    K.inc(q.dma_start(out=bin_sb[:], in_=a_bin[:, :]), "qS")
                    q.wait_ge(K.sem["qG"][0], nG_woT)
                    K.inc(q.dma_start(
                        out=woT_sb[:],
                        in_=woT_bf.rearrange("(cc p) o -> p cc o", p=128)), "qS")
                    K.inc(q.dma_start(out=bout_sb[:], in_=a_bout[:, :]), "qS")
                nS_w = K.n("qS")

                for w in range(2):
                    @blk.sync
                    def _(q):
                        if w == 1:
                            K.w(q, "sV")
                        for st in range(4):
                            base = w * 2048 + 512 * st
                            K.inc(q.dma_start(out=xsel[:, st, :],
                                              in_=x_in[base:base + 512:4, :]), "qS")
                    nS_x = K.n("qS")
                    for st in range(4):
                        ln_chain(xsel[:, st, :], xln[:, st, :], col=st,
                                 qwaits=[("qS", nS_x)] if st == 0 else [])
                    for st in range(4):
                        @blk.tensor
                        def _(t, st=st):
                            K.w(t, "sV")
                            if st == 0 and w == 0:
                                K.w(t, "sG")
                            ins = None
                            for fc in range(8):
                                bank = pb[4 + (fc // 4)]
                                ins = t.transpose(
                                    bank[:, (fc % 4) * 128:(fc % 4 + 1) * 128],
                                    xln[:, st, fc * 128:(fc + 1) * 128], eye[:])
                            K.inc(ins, "sT")

                        @blk.vector
                        def _(v, st=st):
                            K.w(v, "sT")
                            ins = None
                            for fc in range(8):
                                bank = pb[4 + (fc // 4)]
                                ins = v.tensor_copy(
                                    xselT[:, fc, st * 128:(st + 1) * 128],
                                    bank[:, (fc % 4) * 128:(fc % 4 + 1) * 128])
                            K.inc(ins, "sV")

                    for et in range(24):
                        @blk.tensor
                        def _(t, et=et):
                            if et == 0:
                                K.w(t, "sV")
                                K.w(t, "qS", nS_w)
                            if et >= 4:
                                K.w(t, "sV")
                            bank = pb[et % 4]
                            ins = None
                            for cc in range(8):
                                ins = t.matmul(bank[:, :],
                                               wiT_sb[:, cc, et * 128:(et + 1) * 128],
                                               xselT[:, cc, :],
                                               start=(cc == 0), stop=(cc == 7))
                            K.inc(ins, "sT")

                        @blk.vector
                        def _(v, et=et):
                            K.w(v, "sT")
                            scale2 = 0.125 if et < 8 else 1.0
                            K.inc(v.tensor_scalar(qkvT[:, et, :], pb[et % 4][:, :],
                                                  bin_sb[:, et:et + 1], float(scale2),
                                                  ALU.add, ALU.mult), "sV")

                    for h in range(16):
                        etq = h // 2
                        pr = 64 * (h % 2)

                        @blk.tensor
                        def _(t, etq=etq, pr=pr, h=h):
                            K.w(t, "sV")
                            ins = None
                            for kc in range(4):
                                ins = t.transpose(
                                    pbb[:, 512 + 64 * kc:512 + 64 * (kc + 1)],
                                    qkvT[pr:pr + 64, 16 + etq, kc * 128:(kc + 1) * 128],
                                    eyeb[pr:pr + 64, pr:pr + 64])
                            K.inc(ins, "sT")

                        @blk.vector
                        def _(v):
                            K.w(v, "sT")
                            ins = None
                            for kc in range(4):
                                ins = v.tensor_copy(
                                    vtm[:, kc, :],
                                    pbb[:, 512 + 64 * kc:512 + 64 * (kc + 1)])
                            K.inc(ins, "sV")

                        for qi in range(4):
                            wid = 128 * (qi + 1)

                            @blk.tensor
                            def _(t, qi=qi, wid=wid, etq=etq, pr=pr):
                                K.w(t, "sV")
                                K.inc(t.matmul(
                                    pb[4][:, 0:wid],
                                    qkvT[pr:pr + 64, etq, qi * 128:(qi + 1) * 128],
                                    qkvT[pr:pr + 64, 8 + etq, 0:wid],
                                    start=True, stop=True), "sT")

                            @blk.vector
                            def _(v, qi=qi, wid=wid):
                                K.w(v, "sT")
                                v.tensor_copy(sc[:, 0:wid], pb[4][:, 0:wid])
                                v.tensor_tensor(sc[:, qi * 128:wid], sc[:, qi * 128:wid],
                                                maskc[:], ALU.add)
                                v.drain()
                                v.tensor_reduce(rr[:, 0:1], sc[:, 0:wid], axis=AX.X,
                                                op=ALU.max)
                                v.drain()
                                K.inc(v.tensor_scalar(rr[:, 1:2], rr[:, 0:1],
                                                      float(-1.0), None, ALU.mult,
                                                      ALU.bypass), "sV")

                            @blk.scalar
                            def _(a, wid=wid):
                                K.w(a, "sV")
                                K.inc(a.activation(out=ptmp[:, 0:wid], in_=sc[:, 0:wid],
                                                   func=AF.Exp, bias=rr[:, 1:2],
                                                   scale=1.0, accum_out=rr[:, 2:3]),
                                      "sA")

                            @blk.vector
                            def _(v, wid=wid):
                                K.w(v, "sA")
                                v.reciprocal(rr[:, 3:4], rr[:, 2:3])
                                v.drain()
                                K.inc(v.tensor_scalar(pbf[:, 0:wid], ptmp[:, 0:wid],
                                                      rr[:, 3:4], None, ALU.mult,
                                                      ALU.bypass), "sV")

                            @blk.tensor
                            def _(t, qi=qi):
                                K.w(t, "sV")
                                ins = None
                                for kc in range(qi + 1):
                                    ins = t.transpose(pbb[:, kc * 128:(kc + 1) * 128],
                                                      pbf[:, kc * 128:(kc + 1) * 128],
                                                      eyeb[:])
                                K.inc(ins, "sT")

                            @blk.vector
                            def _(v, qi=qi):
                                K.w(v, "sT")
                                ins = None
                                for kc in range(qi + 1):
                                    ins = v.tensor_copy(
                                        pT[:, kc, qi * 128:(qi + 1) * 128],
                                        pbb[:, kc * 128:(kc + 1) * 128])
                                K.inc(ins, "sV")

                        @blk.tensor
                        def _(t):
                            K.w(t, "sV")
                            ins = None
                            for kc in range(4):
                                ins = t.matmul(pb[5][0:64, :], vtm[:, kc, :],
                                               pT[:, kc, :],
                                               start=(kc == 0), stop=(kc == 3))
                            K.inc(ins, "sT")

                        @blk.vector
                        def _(v, etq=etq, pr=pr):
                            K.w(v, "sT")
                            K.inc(v.tensor_copy(attnT[pr:pr + 64, etq, :],
                                                pb[5][0:64, :]), "sV")

                    for ot in range(8):
                        @blk.tensor
                        def _(t, ot=ot):
                            K.w(t, "sV")
                            bank = pb[ot % 2]
                            ins = None
                            for cc in range(8):
                                ins = t.matmul(bank[:, :],
                                               woT_sb[:, cc, ot * 128:(ot + 1) * 128],
                                               attnT[:, cc, :], start=(cc == 0),
                                               stop=(cc == 7))
                            K.inc(ins, "sT")

                        @blk.vector
                        def _(v, ot=ot):
                            K.w(v, "sT")
                            K.inc(v.tensor_scalar(aoutT[:, ot, :], pb[ot % 2][:, :],
                                                  bout_sb[:, ot:ot + 1], None, ALU.add,
                                                  ALU.bypass), "sV")

                    for st in range(4):
                        @blk.tensor
                        def _(t, st=st):
                            K.w(t, "sV")
                            ins = None
                            for fc in range(8):
                                bank = pb[4 + (fc // 4)]
                                ins = t.transpose(
                                    bank[:, (fc % 4) * 128:(fc % 4 + 1) * 128],
                                    aoutT[:, fc, st * 128:(st + 1) * 128], eye[:])
                            K.inc(ins, "sT")

                        @blk.vector
                        def _(v, st=st):
                            K.w(v, "sT")
                            K.w(v, "qS")
                            ins = None
                            for fc in range(8):
                                bank = pb[4 + (fc // 4)]
                                ins = v.tensor_copy(
                                    aout[:, fc * 128:(fc + 1) * 128],
                                    bank[:, (fc % 4) * 128:(fc % 4 + 1) * 128])
                            K.inc(ins, "sV")

                        ln_chain(aout[:, :], lnout[:, :], col=4 + st)

                        @blk.vector
                        def _(v, st=st):
                            K.inc(v.tensor_tensor(upd[:], lnout[:, :], xsel[:, st, :],
                                                  ALU.add), "sV")

                        @blk.sync
                        def _(q, st=st, w=w):
                            K.w(q, "sV")
                            q.wait_ge(K.sem["qS"][0], nS_xr)
                            base = w * 2048 + 512 * st
                            K.inc(q.dma_start(out=xr[base:base + 512:4, :], in_=upd[:]),
                                  "qS")
        if stage >= 1:
            p1()

        def p2():
            phase_barrier()
            with ExitStack() as s2:
                sb2 = lambda n, s, dt=F32: s2.enter_context(nc.sbuf_tensor(n, s, dt))
                exp32_sb = sb2("p2_exp32", [32, 1024])
                exp96_sb = sb2("p2_exp96", [96, 3072])
                wrow = [sb2(f"p2_wrow{i}", [128, 3072]) for i in range(2)]
                sexp = sb2("p2_sexp", [128, 3072])
                rsexp = sb2("p2_rsexp", [128, 3072])
                q0 = sb2("p2_q0", [128, 3072])
                wdq = sb2("p2_wdq", [128, 3072])
                sc_ = sb2("p2_sc", [128, 96])
                rs_ = sb2("p2_rs", [128, 96])
                scT = sb2("p2_scT", [96, 256])
                wTs = sb2("p2_wTs", [128, 24, 128], BF16)

                @blk.sync
                def _(q):
                    K.inc(q.dma_start(out=exp32_sb[:], in_=c_exp32), "qS")
                    K.inc(q.dma_start(out=exp96_sb[:], in_=c_exp96), "qS")

                def dequant(src, O, I, G, expc, dst):
                    nblk = I // 128
                    for ot in range(O // 128):
                        @blk.sync
                        def _(q, ot=ot):
                            K.w(q, "sV")
                            K.inc(q.dma_start(out=wrow[ot % 2][:, 0:I],
                                              in_=src[ot * 128:(ot + 1) * 128, :]), "qS")

                        @blk.vector
                        def _(v, ot=ot):
                            K.w(v, "qS")
                            v.tensor_reduce(sc_[:, 0:G],
                                            wrow[ot % 2][:, 0:I].rearrange(
                                                "p (g k) -> p g k", k=32),
                                            axis=AX.X, op=ALU.max,
                                            apply_absolute_value=True)
                            v.drain()
                            v.tensor_scalar(sc_[:, 0:G], sc_[:, 0:G], float(1.0 / 7.0),
                                            float(1e-12), ALU.mult, ALU.max)
                            v.drain()
                            K.inc(v.reciprocal(rs_[:, 0:G], sc_[:, 0:G]), "sV")

                        @blk.tensor
                        def _(t, G=G):
                            K.w(t, "sV")
                            t.transpose(pb[6][0:G, 0:128], sc_[:, 0:G], eye[:])
                            K.inc(t.transpose(pb[6][0:G, 128:256], rs_[:, 0:G], eye[:]),
                                  "sT")

                        @blk.vector
                        def _(v, G=G):
                            K.w(v, "sT")
                            v.tensor_copy(scT[0:G, 0:128], pb[6][0:G, 0:128])
                            K.inc(v.tensor_copy(scT[0:G, 128:256], pb[6][0:G, 128:256]),
                                  "sV")

                        for j in range(I // 512):
                            @blk.tensor
                            def _(t, j=j, G=G):
                                K.w(t, "sV")
                                t.matmul(pb[4][:, :], scT[0:G, 0:128],
                                         expc[0:G, j * 512:(j + 1) * 512],
                                         start=True, stop=True)
                                K.inc(t.matmul(pb[5][:, :], scT[0:G, 128:256],
                                               expc[0:G, j * 512:(j + 1) * 512],
                                               start=True, stop=True), "sT")

                            @blk.vector
                            def _(v, j=j):
                                K.w(v, "sT")
                                v.tensor_copy(sexp[:, j * 512:(j + 1) * 512], pb[4][:, :])
                                K.inc(v.tensor_copy(rsexp[:, j * 512:(j + 1) * 512],
                                                    pb[5][:, :]), "sV")

                        @blk.vector
                        def _(v, ot=ot, I=I):
                            v.tensor_tensor(q0[:, 0:I], wrow[ot % 2][:, 0:I],
                                            rsexp[:, 0:I], ALU.mult)
                            v.tensor_scalar(q0[:, 0:I], q0[:, 0:I], MAGIC, None,
                                            ALU.add, ALU.bypass)
                            v.tensor_scalar(q0[:, 0:I], q0[:, 0:I], MAGIC, float(7.0),
                                            ALU.subtract, ALU.min)
                            v.tensor_scalar(q0[:, 0:I], q0[:, 0:I], float(-8.0), None,
                                            ALU.max, ALU.bypass)
                            K.inc(v.tensor_tensor(wdq[:, 0:I], q0[:, 0:I], sexp[:, 0:I],
                                                  ALU.mult), "sV")

                        for g4 in range(nblk // 4):
                            @blk.tensor
                            def _(t, g4=g4):
                                K.w(t, "sV")
                                ins = None
                                for k4 in range(4):
                                    cc = g4 * 4 + k4
                                    ins = t.transpose(
                                        pb[g4 % 2][:, k4 * 128:(k4 + 1) * 128],
                                        wdq[:, cc * 128:(cc + 1) * 128], eye[:])
                                K.inc(ins, "sT")

                            @blk.vector
                            def _(v, g4=g4, ot=ot):
                                K.w(v, "sT")
                                K.w(v, "qS")
                                ins = None
                                for k4 in range(4):
                                    cc = g4 * 4 + k4
                                    ins = v.tensor_copy(
                                        wTs[:, cc, :],
                                        pb[g4 % 2][:, k4 * 128:(k4 + 1) * 128])
                                K.inc(ins, "sV")

                        @blk.sync
                        def _(q, ot=ot, nblk=nblk, dst=dst, O=O):
                            K.w(q, "sV")
                            K.inc(q.dma_start(
                                out=dst.rearrange("(cc p) o -> p cc o", p=128)[
                                    :, 0:nblk, ot * 128:(ot + 1) * 128],
                                in_=wTs[:, 0:nblk, :]), "qS")

                dequant(f_wu, 3072, 1024, 32, exp32_sb, wupT_q)
                dequant(f_wg, 3072, 1024, 32, exp32_sb, wgT_q)
                dequant(f_wd, 1024, 3072, 96, exp96_sb, wdT_q)

            phase_barrier()
            with ExitStack() as s2:
                sb2 = lambda n, s, dt=F32: s2.enter_context(nc.sbuf_tensor(n, s, dt))
                xt = [sb2(f"p2_xt{i}", [128, 1024]) for i in range(2)]
                xlnb = sb2("p2_xlnb", [128, 1024])
                amx = sb2("p2_amx", [128, 32])
                qxT = sb2("p2_qxT", [128, 8, 4096], BF16)
                lupt = [sb2(f"p2_lupt{i}", [128, 8, 128], BF16) for i in range(2)]
                lgt = [sb2(f"p2_lgt{i}", [128, 8, 128], BF16) for i in range(2)]
                uprow = [sb2(f"p2_uprow{i}", [128, 4096]) for i in range(2)]
                garow = [sb2(f"p2_garow{i}", [128, 4096]) for i in range(2)]
                amup = sb2("p2_amup", [128, 24])
                amg = sb2("p2_amg", [128, 24])

                for i in range(33):
                    if i < 32:
                        @blk.sync
                        def _(q, i=i):
                            if i >= 2:
                                K.w(q, "sV")
                            K.inc(q.dma_start(out=xt[i % 2][:],
                                              in_=xr[i * 128:(i + 1) * 128, :]), "qS")
                    if i >= 1:
                        ii = i - 1
                        ln_chain(xt[ii % 2][:, :], xlnb[:, :], col=ii,
                                 qwaits=[("qS", K.n("qS"))])

                        @blk.vector
                        def _(v, ii=ii):
                            K.inc(v.tensor_reduce(amx[:, ii:ii + 1], xlnb[:, :],
                                                  axis=AX.X, op=ALU.max,
                                                  apply_absolute_value=True), "sV")
                global_absmax_multi([amx[:, :]], am1_i, am1_o,
                                    [(scal[:, 0:1], scal[:, 1:2])])

                for i in range(33):
                    if i < 32:
                        @blk.sync
                        def _(q, i=i):
                            if i >= 2:
                                K.w(q, "sV")
                            K.inc(q.dma_start(out=xt[i % 2][:],
                                              in_=xr[i * 128:(i + 1) * 128, :]), "qS")
                    if i >= 1:
                        ii = i - 1

                        @blk.vector
                        def _(v, ii=ii):
                            K.w(v, "qS", K.n("qS"))
                            v.tensor_scalar(xlnb[:], xt[ii % 2][:],
                                            negm_s[:, ii:ii + 1], rstd_s[:, ii:ii + 1],
                                            ALU.add, ALU.mult)
                            v.tensor_scalar(xlnb[:], xlnb[:], scal[:, 1:2], MAGIC,
                                            ALU.mult, ALU.add)
                            v.tensor_scalar(xlnb[:], xlnb[:], MAGIC, QMAX,
                                            ALU.subtract, ALU.min)
                            K.inc(v.tensor_scalar(xlnb[:], xlnb[:], float(-32768.0),
                                                  scal[:, 0:1], ALU.max, ALU.mult), "sV")

                        @blk.tensor
                        def _(t):
                            K.w(t, "sV")
                            ins = None
                            for fc in range(8):
                                bank = pb[4 + (fc // 4)]
                                ins = t.transpose(
                                    bank[:, (fc % 4) * 128:(fc % 4 + 1) * 128],
                                    xlnb[:, fc * 128:(fc + 1) * 128], eye[:])
                            K.inc(ins, "sT")

                        @blk.vector
                        def _(v, ii=ii):
                            K.w(v, "sT")
                            ins = None
                            for fc in range(8):
                                bank = pb[4 + (fc // 4)]
                                ins = v.tensor_copy(
                                    qxT[:, fc, ii * 128:(ii + 1) * 128],
                                    bank[:, (fc % 4) * 128:(fc % 4 + 1) * 128])
                            K.inc(ins, "sV")

                for it in range(24):
                    @blk.sync
                    def _(q, it=it):
                        if it >= 2:
                            K.w(q, "sT")
                        K.inc(q.dma_start(
                            out=lupt[it % 2][:],
                            in_=wupT_q.rearrange("(cc p) i -> p cc i", p=128)[
                                :, :, it * 128:(it + 1) * 128]), "qS")
                        K.inc(q.dma_start(
                            out=lgt[it % 2][:],
                            in_=wgT_q.rearrange("(cc p) i -> p cc i", p=128)[
                                :, :, it * 128:(it + 1) * 128]), "qS")

                    for which, lw, row, am, sthbm, qq in [
                            (0, lupt, uprow, amup, st_up, "qC"),
                            (1, lgt, garow, amg, st_gate, "qG")]:
                        for grp in range(2):
                            @blk.tensor
                            def _(t, it=it, which=which, lw=lw, grp=grp):
                                if which == 0 and grp == 0:
                                    K.w(t, "qS")
                                K.w(t, "sV")
                                ins = None
                                for cc in range(8):
                                    for tc4 in range(4):
                                        tc = grp * 4 + tc4
                                        ins = t.matmul(
                                            pb[tc4][:, :], lw[it % 2][:, cc, :],
                                            qxT[:, cc, tc * 512:(tc + 1) * 512],
                                            start=(cc == 0), stop=(cc == 7))
                                K.inc(ins, "sT")

                            @blk.vector
                            def _(v, it=it, row=row, am=am, qq=qq, grp=grp):
                                K.w(v, "sT")
                                K.w(v, qq)
                                ins = None
                                for tc4 in range(4):
                                    tc = grp * 4 + tc4
                                    ins = v.tensor_copy(
                                        row[it % 2][:, tc * 512:(tc + 1) * 512],
                                        pb[tc4][:, :])
                                if grp == 1:
                                    ins = v.tensor_reduce(am[:, it:it + 1],
                                                          row[it % 2][:, :],
                                                          axis=AX.X, op=ALU.max,
                                                          apply_absolute_value=True)
                                K.inc(ins, "sV")

                        if which == 0:
                            @blk.scalar
                            def _(a, it=it):
                                K.w(a, "sV")
                                K.inc(a.dma_start(out=st_up[it * 128:(it + 1) * 128, :],
                                                  in_=uprow[it % 2][:]), "qC")
                        else:
                            @blk.gpsimd
                            def _(g, it=it):
                                K.w(g, "sV")
                                K.inc(g.dma_start(out=st_gate[it * 128:(it + 1) * 128, :],
                                                  in_=garow[it % 2][:]), "qG")

                global_absmax_multi([amup[:, :], amg[:, :]], am2_i, am2_o,
                                    [(scal[:, 2:3], scal[:, 3:4]),
                                     (scal[:, 4:5], scal[:, 5:6])])

            phase_barrier()
            with ExitStack() as s2:
                sb2 = lambda n, s, dt=F32: s2.enter_context(nc.sbuf_tensor(n, s, dt))
                grow = [sb2(f"p2_grow{i}", [128, 4096]) for i in range(2)]
                urow = [sb2(f"p2_urow{i}", [128, 4096]) for i in range(2)]
                g3b = sb2("p2_g3b", [128, 4096])
                sigb = sb2("p2_sigb", [128, 4096])
                b2b = sb2("p2_b2b", [128, 4096])
                orow = [sb2(f"p2_orow{i}", [128, 4096]) for i in range(2)]
                amg2 = sb2("p2_amg2", [128, 24])
                amo = sb2("p2_amo", [128, 24])

                for it in range(24):
                    @blk.scalar
                    def _(a, it=it):
                        if it == 0:
                            K.w(a, "qG")
                        if it >= 2:
                            K.w(a, "sV")
                        K.inc(a.dma_start(out=grow[it % 2][:],
                                          in_=st_gate[it * 128:(it + 1) * 128, :]), "qC")

                    @blk.vector
                    def _(v, it=it):
                        K.w(v, "qC")
                        v.tensor_scalar(g3b[:], grow[it % 2][:], scal[:, 5:6], MAGIC,
                                        ALU.mult, ALU.add)
                        v.tensor_scalar(g3b[:], g3b[:], MAGIC, QMAX,
                                        ALU.subtract, ALU.min)
                        K.inc(v.tensor_scalar(g3b[:], g3b[:], float(-32768.0),
                                              scal[:, 4:5], ALU.max, ALU.mult), "sV")

                    @blk.scalar
                    def _(a):
                        K.w(a, "sV")
                        K.inc(a.activation(out=sigb[:], in_=g3b[:], func=AF.Sigmoid,
                                           scale=1.0), "sA")

                    @blk.vector
                    def _(v, it=it):
                        K.w(v, "sA")
                        K.w(v, "qG")
                        v.tensor_scalar(sigb[:], sigb[:], float(65536.0), MAGIC,
                                        ALU.mult, ALU.add)
                        v.tensor_scalar(sigb[:], sigb[:], MAGIC, float(65535.0),
                                        ALU.subtract, ALU.min)
                        v.tensor_scalar(sigb[:], sigb[:], float(0.0), float(2.0 ** -16),
                                        ALU.max, ALU.mult)
                        v.tensor_tensor(orow[it % 2][:], sigb[:], g3b[:], ALU.mult)
                        K.inc(v.tensor_reduce(amg2[:, it:it + 1], orow[it % 2][:],
                                              axis=AX.X, op=ALU.max,
                                              apply_absolute_value=True), "sV")

                    @blk.gpsimd
                    def _(g, it=it):
                        K.w(g, "sV")
                        K.inc(g.dma_start(out=st_g2[it * 128:(it + 1) * 128, :],
                                          in_=orow[it % 2][:]), "qG")

                global_absmax_multi([amg2[:, :]], am3_i, am3_o,
                                    [(scal2[:, 0:1], scal2[:, 1:2])])

                for it in range(24):
                    @blk.scalar
                    def _(a, it=it):
                        if it == 0:
                            K.w(a, "qG")
                        if it >= 2:
                            K.w(a, "sV")
                        K.inc(a.dma_start(out=grow[it % 2][:],
                                          in_=st_g2[it * 128:(it + 1) * 128, :]), "qC")

                    @blk.sync
                    def _(q, it=it):
                        if it == 0:
                            K.w(q, "qC")
                        if it >= 2:
                            K.w(q, "sV")
                        K.inc(q.dma_start(out=urow[it % 2][:],
                                          in_=st_up[it * 128:(it + 1) * 128, :]), "qS")

                    @blk.vector
                    def _(v, it=it):
                        K.w(v, "qC")
                        K.w(v, "qS")
                        v.tensor_scalar(g3b[:], grow[it % 2][:], scal2[:, 1:2], MAGIC,
                                        ALU.mult, ALU.add)
                        v.tensor_scalar(g3b[:], g3b[:], MAGIC, QMAX,
                                        ALU.subtract, ALU.min)
                        v.tensor_scalar(g3b[:], g3b[:], float(-32768.0),
                                        scal2[:, 0:1], ALU.max, ALU.mult)
                        v.tensor_scalar(b2b[:], urow[it % 2][:], scal[:, 3:4], MAGIC,
                                        ALU.mult, ALU.add)
                        v.tensor_scalar(b2b[:], b2b[:], MAGIC, QMAX,
                                        ALU.subtract, ALU.min)
                        v.tensor_scalar(b2b[:], b2b[:], float(-32768.0),
                                        scal[:, 2:3], ALU.max, ALU.mult)
                        K.w(v, "qG")
                        v.tensor_tensor(orow[it % 2][:], g3b[:], b2b[:], ALU.mult)
                        K.inc(v.tensor_reduce(amo[:, it:it + 1], orow[it % 2][:],
                                              axis=AX.X, op=ALU.max,
                                              apply_absolute_value=True), "sV")

                    @blk.gpsimd
                    def _(g, it=it):
                        K.w(g, "sV")
                        K.inc(g.dma_start(out=st_o[it * 128:(it + 1) * 128, :],
                                          in_=orow[it % 2][:]), "qG")

                global_absmax_multi([amo[:, :]], am4_i, am4_o,
                                    [(scal2[:, 2:3], scal2[:, 3:4])])

            phase_barrier()
            with ExitStack() as s2:
                sb2 = lambda n, s, dt=F32: s2.enter_context(nc.sbuf_tensor(n, s, dt))
                wdT_sb = s2.enter_context(nc.sbuf_tensor("p2_wdT", [128, 24, 1024], BF16))
                oblk = [sb2(f"p2_oblk{i}", [128, 24, 128]) for i in range(2)]
                ob1 = sb2("p2_ob1", [128, 3072])
                obf = s2.enter_context(nc.sbuf_tensor("p2_obf", [128, 24, 128], BF16))
                ffn_t = sb2("p2_ffn_t", [128, 1024])
                lnout2 = sb2("p2_lnout2", [128, 1024])
                xt2 = [sb2(f"p2_xt2{i}", [128, 1024]) for i in range(2)]
                upd2 = [sb2(f"p2_upd2{i}", [128, 1024]) for i in range(2)]

                @blk.sync
                def _(q):
                    K.inc(q.dma_start(
                        out=wdT_sb[:],
                        in_=wdT_q.rearrange("(ic p) h -> p ic h", p=128)), "qS")

                for tt_ in range(32):
                    @blk.scalar
                    def _(a, tt_=tt_):
                        if tt_ == 0:
                            K.w(a, "qG")
                        if tt_ >= 2:
                            K.w(a, "sV")
                        K.inc(a.dma_start(
                            out=oblk[tt_ % 2][:],
                            in_=st_o.rearrange("(ic p) t -> p ic t", p=128)[
                                :, :, tt_ * 128:(tt_ + 1) * 128]), "qC")

                    @blk.sync
                    def _(q, tt_=tt_):
                        if tt_ >= 2:
                            K.w(q, "sV")
                        K.inc(q.dma_start(out=xt2[tt_ % 2][:],
                                          in_=xr[tt_ * 128:(tt_ + 1) * 128, :]), "qS")

                    @blk.vector
                    def _(v, tt_=tt_):
                        K.w(v, "qC")
                        ob = oblk[tt_ % 2][:].rearrange("p a b -> p (a b)")
                        v.tensor_scalar(ob1[:], ob, scal2[:, 3:4], MAGIC,
                                        ALU.mult, ALU.add)
                        v.tensor_scalar(ob1[:], ob1[:], MAGIC, QMAX,
                                        ALU.subtract, ALU.min)
                        K.inc(v.tensor_scalar(
                            obf[:].rearrange("p a b -> p (a b)"), ob1[:],
                            float(-32768.0), scal2[:, 2:3], ALU.max, ALU.mult), "sV")

                    @blk.tensor
                    def _(t, tt_=tt_):
                        K.w(t, "sV")
                        if tt_ == 0:
                            K.w(t, "qS")
                        ins = None
                        for ic in range(24):
                            t.matmul(pb[0][:, :], obf[:, ic, :],
                                     wdT_sb[:, ic, 0:512],
                                     start=(ic == 0), stop=(ic == 23))
                            ins = t.matmul(pb[1][:, :], obf[:, ic, :],
                                           wdT_sb[:, ic, 512:1024],
                                           start=(ic == 0), stop=(ic == 23))
                        K.inc(ins, "sT")

                    @blk.vector
                    def _(v):
                        K.w(v, "sT")
                        v.tensor_copy(ffn_t[:, 0:512], pb[0][:, :])
                        K.inc(v.tensor_copy(ffn_t[:, 512:1024], pb[1][:, :]), "sV")

                    ln_chain(ffn_t[:, :], lnout2[:, :], col=tt_ % 32)

                    @blk.vector
                    def _(v, tt_=tt_):
                        K.w(v, "qS")
                        K.inc(v.tensor_tensor(upd2[tt_ % 2][:], lnout2[:],
                                              xt2[tt_ % 2][:], ALU.add), "sV")

                    @blk.sync
                    def _(q, tt_=tt_):
                        K.w(q, "sV")
                        K.inc(q.dma_start(out=xr[tt_ * 128:(tt_ + 1) * 128, :],
                                          in_=upd2[tt_ % 2][:]), "qS")

        if stage >= 2:
            p2()

        def p3():
            phase_barrier()
            with ExitStack() as s3:
                sb3 = lambda n, s, dt=F32: s3.enter_context(nc.sbuf_tensor(n, s, dt))
                xt3 = [sb3(f"p3_xt{i}", [128, 1024]) for i in range(2)]
                xlnb3 = sb3("p3_xlnb", [128, 1024])
                xlnT = s3.enter_context(nc.sbuf_tensor("p3_xlnT", [128, 8, 4096], BF16))
                for i in range(33):
                    if i < 32:
                        @blk.sync
                        def _(q, i=i):
                            if i >= 2:
                                K.w(q, "sV")
                            K.inc(q.dma_start(out=xt3[i % 2][:],
                                              in_=xr[i * 128:(i + 1) * 128, :]), "qS")
                    if i >= 1:
                        ii = i - 1
                        ln_chain(xt3[ii % 2][:, :], xlnb3[:, :], col=ii,
                                 qwaits=[("qS", K.n("qS"))])

                        @blk.tensor
                        def _(t):
                            K.w(t, "sV")
                            ins = None
                            for fc in range(8):
                                bank = pb[4 + (fc // 4)]
                                ins = t.transpose(
                                    bank[:, (fc % 4) * 128:(fc % 4 + 1) * 128],
                                    xlnb3[:, fc * 128:(fc + 1) * 128], eye[:])
                            K.inc(ins, "sT")

                        @blk.vector
                        def _(v, ii=ii):
                            K.w(v, "sT")
                            ins = None
                            for fc in range(8):
                                bank = pb[4 + (fc // 4)]
                                ins = v.tensor_copy(
                                    xlnT[:, fc, ii * 128:(ii + 1) * 128],
                                    bank[:, (fc % 4) * 128:(fc % 4 + 1) * 128])
                            K.inc(ins, "sV")

                @blk.sync
                def _(q):
                    K.w(q, "sV")
                    K.inc(q.dma_start(
                        out=xT_loc.rearrange("(fc p) t -> p fc t", p=128),
                        in_=xlnT[:, :, :]), "qS")

                @blk.gpsimd
                def _(g):
                    K.w(g, "qS")
                    K.inc(g.collective_compute(
                        "AllGather", ALU.bypass,
                        replica_groups=[[0, 1], [2, 3], [4, 5], [6, 7]],
                        ins=[xT_loc.opt()], outs=[xT_all.opt()]), "sCC")
                    K.w(g, "sCC")
                    K.inc(g.dma_start(out=mark[0:1, :], in_=xT_all[0:1, :]), "qG")

            phase_barrier()
            with ExitStack() as s3:
                sb3 = lambda n, s, dt=F32: s3.enter_context(nc.sbuf_tensor(n, s, dt))
                wb_sb = s3.enter_context(nc.sbuf_tensor("p3_wb", [128, 8, 16], BF16))
                wc_sb = s3.enter_context(nc.sbuf_tensor("p3_wc", [128, 8, 16], BF16))
                bb_sb = sb3("p3_bb", [16, 1])
                bc_sb = sb3("p3_bc", [16, 1])
                eb_sb = s3.enter_context(nc.sbuf_tensor("p3_eb", [16, 2048], BF16))
                xtc = [s3.enter_context(nc.sbuf_tensor(f"p3_xtc{i}", [128, 8, 512], BF16))
                       for i in range(2)]
                BT = s3.enter_context(nc.sbuf_tensor("p3_BT", [16, 8192], BF16))
                CT = s3.enter_context(nc.sbuf_tensor("p3_CT", [16, 8192], BF16))
                bst = [s3.enter_context(nc.sbuf_tensor(f"p3_bst{i}", [128, 512], BF16))
                       for i in range(2)]
                cst = [s3.enter_context(nc.sbuf_tensor(f"p3_cst{i}", [128, 512], BF16))
                       for i in range(2)]

                @blk.sync
                def _(q):
                    K.w(q, "qG")
                    K.inc(q.dma_start(
                        out=wb_sb[:], in_=wbT_bf.rearrange("(cc p) n -> p cc n", p=128)),
                        "qS")
                    K.inc(q.dma_start(
                        out=wc_sb[:], in_=wcT_bf.rearrange("(cc p) n -> p cc n", p=128)),
                        "qS")
                    K.inc(q.dma_start(out=bb_sb[:], in_=s_bb), "qS")
                    K.inc(q.dma_start(out=bc_sb[:], in_=s_bc), "qS")
                    K.inc(q.dma_start(out=eb_sb[:], in_=c_eb), "qS")

                for tc in range(17):
                    if tc < 16:
                        @blk.sync
                        def _(q, tc=tc):
                            if tc >= 2:
                                K.w(q, "sT")
                            if tc < 8:
                                src = xT_all.rearrange("(a p) t -> p a t", p=128)[
                                    :, 0:8, tc * 512:(tc + 1) * 512]
                            else:
                                src = xT_all.rearrange("(a p) t -> p a t", p=128)[
                                    :, 8:16, (tc - 8) * 512:(tc - 7) * 512]
                            K.inc(q.dma_start(out=xtc[tc % 2][:], in_=src), "qS")
                    if tc >= 1:
                        ii = tc - 1

                        @blk.tensor
                        def _(t, ii=ii):
                            K.w(t, "qS", K.n("qS"))
                            K.w(t, "sV")
                            ins = None
                            for cc in range(8):
                                t.matmul(pb[2][0:16, :], wb_sb[:, cc, :],
                                         xtc[ii % 2][:, cc, :],
                                         start=(cc == 0), stop=(cc == 7))
                                ins = t.matmul(pb[3][0:16, :], wc_sb[:, cc, :],
                                               xtc[ii % 2][:, cc, :],
                                               start=(cc == 0), stop=(cc == 7))
                            K.inc(ins, "sT")

                        @blk.vector
                        def _(v, ii=ii):
                            K.w(v, "sT")
                            v.tensor_scalar(BT[:, ii * 512:(ii + 1) * 512],
                                            pb[2][0:16, :], bb_sb[:, 0:1], None,
                                            ALU.add, ALU.bypass)
                            K.inc(v.tensor_scalar(CT[:, ii * 512:(ii + 1) * 512],
                                                  pb[3][0:16, :], bc_sb[:, 0:1], None,
                                                  ALU.add, ALU.bypass), "sV")

                for n in range(16):
                    for tc in range(16):
                        @blk.tensor
                        def _(t, n=n, tc=tc):
                            K.w(t, "sV")
                            if n + tc > 0:
                                K.w(t, "sA")
                            t.matmul(pb[4][:, :], eb_sb[:, n * 128:(n + 1) * 128],
                                     BT[0:16, tc * 512:(tc + 1) * 512],
                                     start=True, stop=True)
                            K.inc(t.matmul(pb[5][:, :], eb_sb[:, n * 128:(n + 1) * 128],
                                           CT[0:16, tc * 512:(tc + 1) * 512],
                                           start=True, stop=True), "sT")

                        @blk.scalar
                        def _(a, n=n, tc=tc):
                            K.w(a, "sT")
                            K.w(a, "qC")
                            a.copy(bst[tc % 2][:], pb[4][:, :])
                            K.inc(a.copy(cst[tc % 2][:], pb[5][:, :]), "sA")
                            K.w(a, "sA")
                            K.inc(a.dma_start(out=brep_all[n, :, tc * 512:(tc + 1) * 512],
                                              in_=bst[tc % 2][:]), "qC")
                            K.inc(a.dma_start(out=crep_all[n, :, tc * 512:(tc + 1) * 512],
                                              in_=cst[tc % 2][:]), "qC")

            phase_barrier()
            with ExitStack() as s3:
                sb3 = lambda n, s, dt=F32: s3.enter_context(nc.sbuf_tensor(n, s, dt))
                wdl = s3.enter_context(nc.sbuf_tensor("p3_wdl", [128, 8, 128], BF16))
                xtc = [s3.enter_context(nc.sbuf_tensor(f"p3_xc{i}", [128, 8, 512], BF16))
                       for i in range(2)]
                deltaT = s3.enter_context(nc.sbuf_tensor("p3_deltaT", [128, 8192], BF16))
                spb = sb3("p3_spb", [128, 512])
                xmine = s3.enter_context(nc.sbuf_tensor("p3_xmine", [128, 8192], BF16))
                dx = s3.enter_context(nc.sbuf_tensor("p3_dx", [128, 8192], BF16))
                diag = s3.enter_context(nc.sbuf_tensor("p3_diag", [128, 16, 128], BF16))
                brc = [s3.enter_context(nc.sbuf_tensor(f"p3_brc{i}", [128, 512], BF16))
                       for i in range(2)]
                crc = [s3.enter_context(nc.sbuf_tensor(f"p3_crc{i}", [128, 512], BF16))
                       for i in range(2)]
                abar = [sb3(f"p3_ab{i}", [128, 512]) for i in range(2)]
                bx = [sb3(f"p3_bx{i}", [128, 512]) for i in range(2)]
                hbuf = [sb3(f"p3_h{i}", [128, 512]) for i in range(2)]
                hc = [s3.enter_context(nc.sbuf_tensor(f"p3_hc{i}", [128, 512], BF16))
                      for i in range(2)]
                states = sb3("p3_states", [128, 16])
                xmidx = s3.enter_context(nc.sbuf_tensor("p3_xmidx", [128, 2], U32))
                xmidxf = sb3("p3_xmidxf", [128, 2])
                ysb = sb3("p3_ysb", [128, 512])
                ytst = sb3("p3_ytst", [128, 4, 128])

                for dt in range(4):
                    @blk.sync
                    def _(q, dt=dt):
                        K.w(q, "sT")
                        K.inc(q.dma_start(
                            out=wdl[:],
                            in_=wdTh_bf.rearrange("(cc p) m -> p cc m", p=128)[
                                :, :, dt * 128:(dt + 1) * 128]), "qS")

                    @blk.vector
                    def _(v, dt=dt):
                        K.w(v, "sG")
                        v.memset(states[:], 0.0)
                        v.tensor_scalar(xmidxf[:, 0:1], xtbase[:, 0:1], float(dt * 128),
                                        None, ALU.add, ALU.bypass)
                        v.tensor_scalar(xmidxf[:, 1:2], xtbase[:, 0:1],
                                        float(dt * 128 + 1024), None, ALU.add, ALU.bypass)
                        v.drain()
                        K.inc(v.tensor_copy(xmidx[:], xmidxf[:]), "sV")

                    @blk.gpsimd
                    def _(g, dt=dt):
                        K.w(g, "sV")
                        K.inc(g.indirect_dma_start(
                            out=xmine[:, 0:4096], out_offset=None,
                            in_=xT_all,
                            in_offset=IndirectOffsetOnAxis(ap=xmidx[:, 0:1], axis=0)),
                            "qG")
                        K.inc(g.indirect_dma_start(
                            out=xmine[:, 4096:8192], out_offset=None,
                            in_=xT_all,
                            in_offset=IndirectOffsetOnAxis(ap=xmidx[:, 1:2], axis=0)),
                            "qG")

                    for tc in range(17):
                        if tc < 16:
                            @blk.sync
                            def _(q, tc=tc, dt=dt):
                                if tc >= 2:
                                    K.w(q, "sT")
                                if tc < 8:
                                    src = xT_all.rearrange("(a p) t -> p a t", p=128)[
                                        :, 0:8, tc * 512:(tc + 1) * 512]
                                else:
                                    src = xT_all.rearrange("(a p) t -> p a t", p=128)[
                                        :, 8:16, (tc - 8) * 512:(tc - 7) * 512]
                                K.inc(q.dma_start(out=xtc[tc % 2][:], in_=src), "qS")
                        if tc >= 1:
                            ii = tc - 1

                            @blk.tensor
                            def _(t, ii=ii):
                                K.w(t, "qS", K.n("qS"))
                                K.w(t, "sA")
                                ins = None
                                for cc in range(8):
                                    ins = t.matmul(pb[0][:, :], wdl[:, cc, :],
                                                   xtc[ii % 2][:, cc, :],
                                                   start=(cc == 0), stop=(cc == 7))
                                K.inc(ins, "sT")

                            @blk.scalar
                            def _(a, ii=ii, dt=dt):
                                K.w(a, "sT")
                                a.activation(out=spb[:], in_=pb[0][:, :], func=AF.Exp,
                                             bias=bd_sb[:, dt:dt + 1], scale=1.0)
                                K.inc(a.activation(
                                    out=deltaT[:, ii * 512:(ii + 1) * 512],
                                    in_=spb[:], func=AF.Ln, bias=1.0, scale=1.0), "sA")

                    @blk.vector
                    def _(v, dt=dt):
                        K.w(v, "sA")
                        K.w(v, "qG")
                        K.inc(v.tensor_tensor(dx[:], deltaT[:], xmine[:], ALU.mult), "sV")

                    @blk.vector
                    def _(v, dt=dt):
                        ins = None
                        for n in range(16):
                            ins = v.tensor_scalar(diag[:, n, :], eyeb[:],
                                                  A_sb[:, dt, n:n + 1], None,
                                                  ALU.mult, ALU.bypass)
                        K.inc(ins, "sV")

                    pend_state = [None]
                    for tc in range(16):
                        for n in range(16):
                            i2 = (tc * 16 + n) % 2

                            @blk.sync
                            def _(q, n=n, tc=tc, i2=i2):
                                K.w(q, "sV")
                                K.inc(q.dma_start(
                                    out=brc[i2][:],
                                    in_=brep_all[n, :, tc * 512:(tc + 1) * 512]), "qS")

                            @blk.scalar
                            def _(a, n=n, tc=tc, i2=i2):
                                K.w(a, "sG")
                                K.inc(a.dma_start(
                                    out=crc[i2][:],
                                    in_=crep_all[n, :, tc * 512:(tc + 1) * 512]), "qC")

                            @blk.tensor
                            def _(t, n=n, tc=tc, i2=i2):
                                if tc == 0 and n == 0:
                                    K.w(t, "sV")
                                K.w(t, "sA")
                                K.inc(t.matmul(pb[4 + i2][:, :], diag[:, n, :],
                                               deltaT[:, tc * 512:(tc + 1) * 512],
                                               start=True, stop=True), "sT")

                            @blk.scalar
                            def _(a, i2=i2):
                                K.w(a, "sT")
                                K.inc(a.activation(out=abar[i2][:], in_=pb[4 + i2][:, :],
                                                   func=AF.Exp, scale=1.0), "sA")

                            @blk.vector
                            def _(v, n=n, tc=tc, i2=i2):
                                K.w(v, "sA")
                                K.w(v, "qS", K.n("qS"))
                                v.tensor_tensor(bx[i2][:], dx[:, tc * 512:(tc + 1) * 512],
                                                brc[i2][:], ALU.mult)
                                if pend_state[0] is not None:
                                    pn, pi = pend_state[0]
                                    v.tensor_copy(states[:, pn:pn + 1],
                                                  hbuf[pi][:, 511:512])
                                K.w(v, "sG")
                                K.inc(v.tensor_tensor_scan(
                                    hbuf[i2][:], abar[i2][:], bx[i2][:],
                                    states[:, n:n + 1], ALU.mult, ALU.add), "sV")
                                pend_state[0] = (n, i2)

                            @blk.gpsimd
                            def _(g, i2=i2):
                                K.w(g, "sV")
                                K.w(g, "qC", K.n("qC"))
                                K.w(g, "sT")
                                K.inc(g.tensor_tensor(hc[i2][:], hbuf[i2][:],
                                                      crc[i2][:], ALU.mult), "sG")

                            @blk.tensor
                            def _(t, n=n, i2=i2):
                                K.w(t, "sG")
                                K.inc(t.matmul(pb[6][:, :], eyeb[:], hc[i2][:],
                                               start=(n == 0), stop=(n == 15)), "sT")

                        @blk.scalar
                        def _(a, tc=tc):
                            K.w(a, "sT")
                            K.inc(a.copy(ysb[:], pb[6][:, :]), "sA")

                        @blk.tensor
                        def _(t, tc=tc):
                            K.w(t, "sA")
                            K.w(t, "sV")
                            ins = None
                            for tb in range(4):
                                ins = t.transpose(pb[2][:, tb * 128:(tb + 1) * 128],
                                                  ysb[:, tb * 128:(tb + 1) * 128], eye[:])
                            K.inc(ins, "sT")

                        @blk.vector
                        def _(v, tc=tc):
                            K.w(v, "sT")
                            K.w(v, "qG", K.n("qG"))
                            ins = None
                            for tb in range(4):
                                ins = v.tensor_copy(ytst[:, tb, :],
                                                    pb[2][:, tb * 128:(tb + 1) * 128])
                            K.inc(ins, "sV")

                        @blk.gpsimd
                        def _(g, tc=tc, dt=dt):
                            K.w(g, "sV")
                            K.inc(g.dma_start(
                                out=ytm_loc.rearrange(
                                    "(a tb p) (d j) -> p a tb d j", tb=4, p=128, j=128)[
                                    :, tc, :, dt, :],
                                in_=ytst[:, :, :]), "qG")

            @blk.gpsimd
            def _(g):
                K.w(g, "qG")
                K.inc(g.collective_compute(
                    "AllGather", ALU.bypass,
                    replica_groups=[[0, 1], [2, 3], [4, 5], [6, 7]],
                    ins=[ytm_loc.opt()], outs=[ytm_all.opt()]), "sCC")
                K.w(g, "sCC")

            phase_barrier()
            with ExitStack() as s3:
                sb3 = lambda n, s, dt=F32: s3.enter_context(nc.sbuf_tensor(n, s, dt))
                yfull = [sb3(f"p3_yf{i}", [128, 1024]) for i in range(2)]
                xt4 = [sb3(f"p3_xt4{i}", [128, 1024]) for i in range(2)]
                lnout3 = sb3("p3_lnout", [128, 1024])
                upd3 = [sb3(f"p3_upd{i}", [128, 1024]) for i in range(2)]
                yidxf = sb3("p3_yidxf", [128, 2])
                yidx = s3.enter_context(nc.sbuf_tensor("p3_yidx", [128, 32, 2], U32))

                @blk.vector
                def _(v):
                    K.w(v, "qG")
                    ins = None
                    for i in range(32):
                        v.tensor_scalar(yidxf[:, 0:1], ytbase[:, 0:1], float(i * 128),
                                        None, ALU.add, ALU.bypass)
                        v.tensor_scalar(yidxf[:, 1:2], ytbase[:, 0:1],
                                        float(i * 128 + 8192), None, ALU.add, ALU.bypass)
                        v.drain()
                        ins = v.tensor_copy(yidx[:, i, :], yidxf[:])
                    K.inc(ins, "sV")

                for i in range(33):
                    if i < 32:
                        @blk.gpsimd
                        def _(g, i=i):
                            if i == 0:
                                K.w(g, "sV")
                            if i >= 2:
                                K.w(g, "sV")
                            K.inc(g.indirect_dma_start(
                                out=yfull[i % 2][:, 0:512], out_offset=None,
                                in_=ytm_all,
                                in_offset=IndirectOffsetOnAxis(ap=yidx[:, i, 0:1],
                                                               axis=0)), "qG")
                            K.inc(g.indirect_dma_start(
                                out=yfull[i % 2][:, 512:1024], out_offset=None,
                                in_=ytm_all,
                                in_offset=IndirectOffsetOnAxis(ap=yidx[:, i, 1:2],
                                                               axis=0)), "qG")

                        @blk.sync
                        def _(q, i=i):
                            if i >= 2:
                                K.w(q, "sV")
                            K.inc(q.dma_start(out=xt4[i % 2][:],
                                              in_=xr[i * 128:(i + 1) * 128, :]), "qS")
                    if i >= 1:
                        ii = i - 1
                        ln_chain(yfull[ii % 2][:, :], lnout3[:, :], col=ii,
                                 qwaits=[("qG", K.n("qG")), ("qS", K.n("qS"))])

                        @blk.vector
                        def _(v, ii=ii):
                            K.w(v, "qS", K.n("qS"))
                            K.inc(v.tensor_tensor(upd3[ii % 2][:], lnout3[:],
                                                  xt4[ii % 2][:], ALU.add), "sV")

                        @blk.sync
                        def _(q, ii=ii):
                            K.w(q, "sV")
                            K.inc(q.dma_start(out=xr[ii * 128:(ii + 1) * 128, :],
                                              in_=upd3[ii % 2][:]), "qS")

        if stage >= 3:
            p3()

        nonlocal_rt = [None]

        def p4(sub=99, r1sub=99, r2sub=99):
            phase_barrier()
            with ExitStack() as s4:
                sb4 = lambda n, s, dt=F32: s4.enter_context(nc.sbuf_tensor(n, s, dt))
                xt5 = [sb4(f"p4_xt{i}", [128, 1024]) for i in range(2)]
                hin = sb4("p4_hin", [128, 1024])
                RT = sb4("p4_RT", [128, 32, 8])
                nonlocal_rt[0] = RT
                idxf = sb4("p4_idxf", [128, 2])
                idxu = s4.enter_context(nc.sbuf_tensor("p4_idxu", [128, 2], U32))
                src4 = sb4("p4_src4", [128, 4])
                tl1 = sb4("p4_tl1", [128, 1])
                tl2 = sb4("p4_tl2", [128, 1])
                sg_sb = sb4("p4_sg", [128, 512])
                hact = sb4("p4_hact", [128, 512])
                actT = s4.enter_context(nc.sbuf_tensor("p4_actT", [128, 4, 4096], BF16))
                swd_sb = s4.enter_context(nc.sbuf_tensor("p4_swd", [128, 4, 1024], BF16))
                s4r = s4.enter_context(ExitStack())
                sbr = lambda n, s, dt=F32: s4r.enter_context(nc.sbuf_tensor(n, s, dt))
                xnf = sbr("p4_xnf", [128, 1024])
                xnb_t = s4r.enter_context(nc.sbuf_tensor("p4_xnb", [128, 1024], BF16))
                xnT_bf = s4r.enter_context(nc.sbuf_tensor("p4_xnT", [128, 8, 4096], BF16))
                xnTf = [sbr(f"p4_xnTf{i}", [128, 8, 128]) for i in range(2)]
                gwT_sb = sbr("p4_gwT", [128, 8, 64])
                Lt = sbr("p4_Lt", [128, 64])
                top8 = sbr("p4_top8", [128, 8])
                top8i = s4r.enter_context(nc.sbuf_tensor("p4_top8i", [128, 8], U32))
                rrt = sbr("p4_rrt", [128, 8])
                bases = sbr("p4_bases", [1, 2048])
                runrow = sbr("p4_runrow", [1, 64])
                cnt_sb = sbr("p4_cnt", [8, 64])
                prevb = sbr("p4_prevb", [1, 64])
                f0b = s4r.enter_context(nc.sbuf_tensor("p4_f0b", [128, 64], BF16))
                f1b = s4r.enter_context(nc.sbuf_tensor("p4_f1b", [128, 64], BF16))
                f01 = s4r.enter_context(nc.sbuf_tensor("p4_f01", [128, 64], BF16))
                f0f = sbr("p4_f0f", [128, 64])
                f1f = sbr("p4_f1f", [128, 64])
                Lm = sbr("p4_Lm", [128, 64])
                prefs = sbr("p4_prefs", [128, 64])
                prefg = sbr("p4_prefg", [128, 64])
                fsel = sbr("p4_fsel", [128, 64])
                rk1 = sbr("p4_rk1", [128, 2])
                zero_sb = sbr("p4_zero", [128, 1024])
                zero_b = s4r.enter_context(nc.sbuf_tensor("p4_zerob", [128, 1024], BF16))
                swg_sb = s4r.enter_context(nc.sbuf_tensor("p4_swg", [128, 8, 512], BF16))
                swu_sb = s4r.enter_context(nc.sbuf_tensor("p4_swu", [128, 8, 512], BF16))

                @blk.gpsimd
                def _(g):
                    g.memset(zero_sb[:], 0.0)
                    g.memset(zero_b[:], 0.0)
                    g.memset(runrow[:], 0.0)
                    K.inc(g.memset(src4[:, 1:4], 0.0), "sG")
                    K.w(g, "sG")
                    K.inc(g.dma_start(out=tok_i[0:32768, :], in_=zero_sb[:, :]), "qG")
                    K.inc(g.dma_start(out=tok_i[32768:65536, :], in_=zero_sb[:, :]), "qG")
                    K.inc(g.dma_start(out=tok_i[65536:65600, :], in_=zero_sb[0:1, 0:256]),
                          "qG")


                @blk.sync
                def _(q):
                    K.inc(q.dma_start(
                        out=gwT_sb[:],
                        in_=m_gwT.rearrange("(cc p) e -> p cc e", p=128)), "qS")

                for i in range(32):
                    @blk.sync
                    def _(q, i=i):
                        if i >= 2:
                            K.w(q, "sV")
                        K.inc(q.dma_start(out=xt5[i % 2][:],
                                          in_=xr[i * 128:(i + 1) * 128, :]), "qS")
                    ln_chain(xt5[i % 2][:, :], hin[:, :], col=i,
                             qwaits=[("qS", K.n("qS"))], rms_col=i)

                    @blk.vector
                    def _(v, i=i):
                        v.tensor_scalar(xnf[:], hin[:], rmsr_s[:, i:i + 1], None,
                                        ALU.mult, ALU.bypass)
                        v.tensor_tensor(xnf[:], xnf[:], rmsw_sb[:], ALU.mult)
                        K.w(v, "qC")
                        K.inc(v.tensor_copy(xnb_t[:], xnf[:]), "sV")

                    @blk.scalar
                    def _(a, i=i):
                        K.w(a, "sV")
                        K.inc(a.dma_start(out=xn_loc[i * 128:(i + 1) * 128, :],
                                          in_=xnb_t[:]), "qC")

                    if r1sub < 2:
                        continue

                    @blk.tensor
                    def _(t, i=i):
                        K.w(t, "sV")
                        K.w(t, "sA")
                        ins = None
                        for fc in range(8):
                            bank = pb[4 + (fc // 4)]
                            ins = t.transpose(bank[:, (fc % 4) * 128:(fc % 4 + 1) * 128],
                                              xnf[:, fc * 128:(fc + 1) * 128], eye[:])
                        K.inc(ins, "sT")

                    @blk.vector
                    def _(v, i=i):
                        K.w(v, "sT")
                        ins = None
                        for fc in range(8):
                            bank = pb[4 + (fc // 4)]
                            ins = v.tensor_copy(xnT_bf[:, fc, i * 128:(i + 1) * 128],
                                                bank[:, (fc % 4) * 128:(fc % 4 + 1) * 128])
                        K.inc(ins, "sV")

                    @blk.scalar
                    def _(a, i=i):
                        K.w(a, "sT")
                        ins = None
                        for fc in range(8):
                            bank = pb[4 + (fc // 4)]
                            ins = a.copy(xnTf[i % 2][:, fc, :],
                                         bank[:, (fc % 4) * 128:(fc % 4 + 1) * 128])
                        K.inc(ins, "sA")

                    if r1sub < 3:
                        continue

                    @blk.tensor
                    def _(t, i=i):
                        K.w(t, "sA")
                        ins = None
                        for cc in range(8):
                            ins = t.matmul(pb[6][:, 0:64], xnTf[i % 2][:, cc, :],
                                           gwT_sb[:, cc, :],
                                           start=(cc == 0), stop=(cc == 7))
                        K.inc(ins, "sT")

                    @blk.scalar
                    def _(a):
                        K.w(a, "sT")
                        K.w(a, "sV")
                        K.inc(a.copy(Lt[:], pb[6][:, 0:64]), "sA")

                    if r1sub < 4:
                        continue

                    @blk.vector
                    def _(v, i=i):
                        K.w(v, "sA")
                        v.max_with_indices(top8[:], top8i[:], Lt[:])
                        v.drain()
                        v.tensor_scalar(f0f[:], Lt[:], top8[:, 0:1], None,
                                        ALU.is_equal, ALU.bypass)
                        v.drain()
                        v.scalar_tensor_tensor(Lm[:], f0f[:], float(-1e30), Lt[:],
                                               ALU.mult, ALU.add)
                        v.drain()
                        v.tensor_scalar(f1f[:], Lm[:], top8[:, 1:2], None,
                                        ALU.is_equal, ALU.bypass)
                        v.tensor_tensor(fsel[:], f0f[:], iota64[:], ALU.mult)
                        v.drain()
                        v.tensor_reduce(RT[:, i, 0:1], fsel[:], axis=AX.X, op=ALU.add)
                        v.tensor_tensor(fsel[:], f1f[:], iota64[:], ALU.mult)
                        v.drain()
                        v.tensor_reduce(RT[:, i, 1:2], fsel[:], axis=AX.X, op=ALU.add)
                        v.tensor_tensor(rrt[:, 0:1], top8[:, 0:1], top8[:, 1:2],
                                        ALU.subtract)
                        v.drain()
                        K.inc(v.tensor_copy(rrt[:, 2:3], rrt[:, 0:1]), "sV")

                    @blk.scalar
                    def _(a):
                        K.w(a, "sV")
                        K.inc(a.activation(out=rrt[:, 1:2], in_=rrt[:, 0:1],
                                           func=AF.Sigmoid, scale=1.0), "sA")

                    @blk.vector
                    def _(v, i=i):
                        K.w(v, "sA")
                        K.w(v, "sT")
                        v.tensor_copy(RT[:, i, 2:3], rrt[:, 1:2])
                        v.tensor_scalar(RT[:, i, 3:4], rrt[:, 1:2], float(-1.0),
                                        float(1.0), ALU.mult, ALU.add)
                        v.tensor_tensor(fsel[:], f0f[:], f1f[:], ALU.add)
                        v.drain()
                        K.inc(v.tensor_copy(f01[:], fsel[:]), "sV")

                    if r1sub < 5:
                        continue

                    @blk.tensor
                    def _(t, i=i):
                        K.w(t, "sV")
                        K.inc(t.matmul(pb[3][0:1, 0:64], onescb[:], f01[:],
                                       start=True, stop=True), "sT")

                    @blk.vector
                    def _(v, i=i):
                        K.w(v, "sT")
                        v.tensor_copy(bases[0:1, i * 64:(i + 1) * 64], runrow[0:1, :])
                        v.drain()
                        v.tensor_tensor(runrow[0:1, :], runrow[0:1, :],
                                        pb[3][0:1, 0:64], ALU.add)
                        v.drain()
                        K.inc(v.engine_nop() if False else v.tensor_copy(
                            trow2[0:1, 0:1], runrow[0:1, 0:1]), "sV")

                if sub < 1:
                    return
                @blk.gpsimd
                def _(g):
                    K.w(g, "sV")
                    K.inc(g.dma_start(out=cnt_i, in_=runrow[0:1, :]), "qG")
                    K.w(g, "qG")
                    K.inc(g.collective_compute(
                        "AllGather", ALU.bypass, replica_groups=[list(range(8))],
                        ins=[cnt_i.opt()], outs=[cnt_o.opt()]), "sCC")
                    K.w(g, "sCC")
                    K.inc(g.dma_start(out=cnt_sb[:], in_=cnt_o), "qG")
                    K.w(g, "qC")
                    K.inc(g.collective_compute(
                        "AllGather", ALU.bypass, replica_groups=[list(range(8))],
                        ins=[xn_loc.opt()], outs=[X_all.opt()]), "sCC")

                @blk.tensor
                def _(t):
                    K.w(t, "qG")
                    K.inc(t.matmul(pb[6][0:1, 0:64], mprev[:, :], cnt_sb[:, :],
                                   start=True, stop=True), "sT")

                @blk.vector
                def _(v):
                    K.w(v, "sT")
                    K.inc(v.tensor_copy(prevb[0:1, :], pb[6][0:1, 0:64]), "sV")


                @blk.sync
                def _(q):
                    K.inc(q.dma_start(
                        out=swg_sb[:],
                        in_=swg_bf.rearrange("(cc p) i -> p cc i", p=128)), "qS")
                    K.inc(q.dma_start(
                        out=swu_sb[:],
                        in_=swu_bf.rearrange("(cc p) i -> p cc i", p=128)), "qS")
                    K.inc(q.dma_start(
                        out=swd_sb[:],
                        in_=swd_bf.rearrange("(ic p) h -> p ic h", p=128)), "qS")
                nqs_sw = K.n("qS")

                for it4 in range(4):
                    for tc in range(8):
                        @blk.tensor
                        def _(t, it4=it4, tc=tc):
                            if it4 == 0 and tc == 0:
                                K.w(t, "qS", nqs_sw)
                            K.w(t, "sV")
                            ins = None
                            for cc in range(8):
                                t.matmul(pb[0][:, :], swg_sb[:, cc, it4 * 128:(it4 + 1) * 128],
                                         xnT_bf[:, cc, tc * 512:(tc + 1) * 512],
                                         start=(cc == 0), stop=(cc == 7))
                                ins = t.matmul(pb[1][:, :],
                                               swu_sb[:, cc, it4 * 128:(it4 + 1) * 128],
                                               xnT_bf[:, cc, tc * 512:(tc + 1) * 512],
                                               start=(cc == 0), stop=(cc == 7))
                            K.inc(ins, "sT")

                        @blk.scalar
                        def _(a):
                            K.w(a, "sT")
                            K.inc(a.activation(out=sg_sb[:], in_=pb[0][:, :],
                                               func=AF.Silu, scale=1.0), "sA")

                        @blk.vector
                        def _(v, it4=it4, tc=tc):
                            K.w(v, "sA")
                            v.tensor_tensor(hact[:], sg_sb[:], pb[1][:, :], ALU.mult)
                            K.inc(v.tensor_copy(actT[:, it4, tc * 512:(tc + 1) * 512],
                                                hact[:]), "sV")

                if sub < 2:
                    return
                for i in range(32):
                    @blk.vector
                    def _(v, i=i):
                        K.w(v, "sV")
                        K.w(v, "sT")
                        K.w(v, "qG")
                        v.tensor_tensor(trow2[0:1, 0:64], bases[0:1, i * 64:(i + 1) * 64],
                                        prevb[0:1, :], ALU.add)
                        v.tensor_scalar(f0b[:], iota64[:], RT[:, i, 0:1], None,
                                        ALU.is_equal, ALU.bypass)
                        v.tensor_scalar(f1b[:], iota64[:], RT[:, i, 1:2], None,
                                        ALU.is_equal, ALU.bypass)
                        v.drain()
                        K.inc(v.tensor_tensor(f01[:], f0b[:], f1b[:], ALU.add), "sV")

                    @blk.tensor
                    def _(t, i=i):
                        K.w(t, "sV")
                        t.matmul(pb[4][:, 0:64], striu[:], f01[:], start=True, stop=True)
                        K.inc(t.matmul(pb[5][:, 0:64], onesr[:, :], trow2[0:1, 0:64],
                                       start=True, stop=True), "sT")

                    if r2sub < 2:
                        continue

                    @blk.vector
                    def _(v, i=i):
                        K.w(v, "sT")
                        v.tensor_copy(prefs[:], pb[4][:, 0:64])
                        v.drain()
                        v.tensor_tensor(prefg[:], prefs[:], pb[5][:, 0:64], ALU.add)
                        v.drain()
                        for k in range(2):
                            v.tensor_tensor(fsel[:], prefg[:], f0b[:] if k == 0 else f1b[:],
                                            ALU.mult)
                            v.drain()
                            v.tensor_reduce(rk1[:, k:k + 1], fsel[:], axis=AX.X,
                                            op=ALU.add)
                        v.drain()
                        for k in range(2):
                            v.tensor_scalar(RT[:, i, 4 + k:5 + k], rk1[:, k:k + 1],
                                            float(1024.0), None, ALU.min, ALU.bypass)
                            v.tensor_scalar(RT[:, i, 6 + k:7 + k], rk1[:, k:k + 1],
                                            float(1024.0), None, ALU.is_ge, ALU.bypass)
                        v.drain()
                        for k in range(2):
                            v.tensor_scalar(tl1[:], RT[:, i, 6 + k:7 + k], float(-1.0),
                                            float(1.0), ALU.mult, ALU.add)
                            v.drain()
                            v.tensor_tensor(RT[:, i, 2 + k:3 + k], RT[:, i, 2 + k:3 + k],
                                            tl1[:], ALU.mult)
                        v.drain()
                        for k in range(2):
                            v.scalar_tensor_tensor(idxf[:, k:k + 1],
                                                   RT[:, i, k:k + 1], float(1025.0),
                                                   RT[:, i, 4 + k:5 + k],
                                                   ALU.mult, ALU.add)
                        v.tensor_scalar(src4[:, 0:1], tg1[:, 0:1], float(i * 128),
                                        None, ALU.add, ALU.bypass)
                        v.tensor_scalar(idxf[:], idxf[:], float(65599.0), float(0.0),
                                        ALU.min, ALU.max)
                        v.drain()
                        K.inc(v.tensor_copy(idxu[:], idxf[:]), "sV")

                    if r2sub < 3:
                        continue

                    @blk.gpsimd
                    def _(g, i=i):
                        K.w(g, "sV")
                        K.w(g, "qG", K.n("qG"))
                        for k in range(2):
                            K.inc(g.indirect_dma_start(
                                out=tok_i,
                                out_offset=IndirectOffsetOnAxis(ap=idxu[:, k:k + 1],
                                                                axis=0),
                                in_=src4[:, :], in_offset=None), "qG")

                if r2sub >= 4:
                    @blk.gpsimd
                    def _(g):
                        K.w(g, "qG")
                        K.inc(g.collective_compute(
                            "ReduceScatter", ALU.add, replica_groups=[list(range(8))],
                            ins=[tok_i.opt()], outs=[tok_my.opt()]), "sCC")
                        K.w(g, "sCC")

                if sub < 3:
                    return
                s4r.close()
                phase_barrier()
                s4e = s4.enter_context(ExitStack())
                w13_sb = [s4e.enter_context(nc.sbuf_tensor(f"p4_w13_{i}", [128, 8, 1024],
                                                           BF16)) for i in range(2)]
                w2_sb = [s4e.enter_context(nc.sbuf_tensor(f"p4_w2_{i}", [128, 4, 1024],
                                                          BF16)) for i in range(2)]
                xbT = s4e.enter_context(nc.sbuf_tensor("p4_xbT", [128, 8, 1024], BF16))
                xg = [s4e.enter_context(nc.sbuf_tensor(f"p4_xg{i}", [128, 1024], BF16))
                      for i in range(2)]
                idxe_f = s4e.enter_context(nc.sbuf_tensor("p4_idxef", [128, 8], F32))
                idxe = s4e.enter_context(nc.sbuf_tensor("p4_idxe", [128, 8], U32))
                idxg = s4e.enter_context(nc.sbuf_tensor("p4_idxg", [128, 8], U32))
                tokrow = s4e.enter_context(nc.sbuf_tensor("p4_tokrow", [128, 8, 4], F32))
                hactT = s4e.enter_context(nc.sbuf_tensor("p4_hactT", [128, 4, 1024], BF16))
                yst = [s4e.enter_context(nc.sbuf_tensor(f"p4_yst{i}", [128, 1024], BF16))
                       for i in range(2)]

                for e in range(8):
                    @blk.sync
                    def _(q, e=e):
                        K.w(q, "sT")
                        K.inc(q.dma_start(
                            out=w13_sb[e % 2][:],
                            in_=w13_bf[e * 1024:(e + 1) * 1024, :].rearrange(
                                "(cc p) f -> p cc f", p=128)), "qS")
                        K.inc(q.dma_start(
                            out=w2_sb[e % 2][:],
                            in_=w2_bf[e * 512:(e + 1) * 512, :].rearrange(
                                "(ic p) h -> p ic h", p=128)), "qS")

                    @blk.vector
                    def _(v, e=e):
                        K.w(v, "qG")
                        ins = None
                        for g8 in range(8):
                            ins = v.tensor_scalar(idxe_f[:, g8:g8 + 1], p128[:, 0:1],
                                                  float(e * 1025 + g8 * 128), None,
                                                  ALU.add, ALU.bypass)
                        v.drain()
                        K.inc(v.tensor_copy(idxg[:], idxe_f[:]), "sV")

                    @blk.gpsimd
                    def _(g, e=e):
                        K.w(g, "sV")
                        ins = None
                        for g8 in range(8):
                            ins = g.indirect_dma_start(
                                out=tokrow[:, g8, :], out_offset=None,
                                in_=tok_my,
                                in_offset=IndirectOffsetOnAxis(ap=idxg[:, g8:g8 + 1],
                                                               axis=0))
                            K.inc(ins, "qG")

                    @blk.vector
                    def _(v, e=e):
                        K.w(v, "qG", K.n("qG"))
                        v.tensor_scalar(idxe_f[:], tokrow[:, :, 0], float(1.0),
                                        float(0.0), ALU.subtract, ALU.max)
                        v.drain()
                        K.inc(v.tensor_copy(idxe[:], idxe_f[:]), "sV")

                    for g8 in range(8):
                        @blk.gpsimd
                        def _(g, e=e, g8=g8):
                            K.w(g, "sV")
                            if g8 == 0 and e == 0:
                                K.w(g, "sCC")
                            K.w(g, "sA")
                            K.inc(g.indirect_dma_start(
                                out=xg[g8 % 2][:], out_offset=None,
                                in_=X_all,
                                in_offset=IndirectOffsetOnAxis(ap=idxe[:, g8:g8 + 1],
                                                               axis=0)), "qG")

                        @blk.tensor
                        def _(t, g8=g8):
                            K.w(t, "qG", K.n("qG"))
                            K.w(t, "sA")
                            ins = None
                            for fc in range(8):
                                ins = t.transpose(pbb[:, fc * 128:(fc + 1) * 128],
                                                  xg[g8 % 2][:, fc * 128:(fc + 1) * 128],
                                                  eyeb[:])
                            K.inc(ins, "sT")

                        @blk.scalar
                        def _(a, g8=g8):
                            K.w(a, "sT")
                            ins = None
                            for fc in range(8):
                                ins = a.copy(xbT[:, fc, g8 * 128:(g8 + 1) * 128],
                                             pbb[:, fc * 128:(fc + 1) * 128])
                            K.inc(ins, "sA")

                    for ct in range(8):
                        @blk.tensor
                        def _(t, e=e, ct=ct):
                            K.w(t, "sA")
                            if ct == 0:
                                K.w(t, "qS", K.n("qS"))
                            K.w(t, "sV")
                            ins = None
                            for cc in range(8):
                                t.matmul(pb[0][:, :], xbT[:, cc, ct * 128:(ct + 1) * 128],
                                         w13_sb[e % 2][:, cc, 0:512],
                                         start=(cc == 0), stop=(cc == 7))
                                ins = t.matmul(pb[1][:, :],
                                               xbT[:, cc, ct * 128:(ct + 1) * 128],
                                               w13_sb[e % 2][:, cc, 512:1024],
                                               start=(cc == 0), stop=(cc == 7))
                            K.inc(ins, "sT")

                        @blk.scalar
                        def _(a):
                            K.w(a, "sT")
                            K.inc(a.activation(out=sg_sb[:], in_=pb[0][:, :],
                                               func=AF.Silu, scale=1.0), "sA")

                        @blk.vector
                        def _(v):
                            K.w(v, "sA")
                            K.inc(v.tensor_tensor(hact[:], sg_sb[:], pb[1][:, :],
                                                  ALU.mult), "sV")

                        @blk.tensor
                        def _(t, ct=ct):
                            K.w(t, "sV")
                            K.w(t, "sA")
                            ins = None
                            for ic in range(4):
                                ins = t.transpose(pb[2][:, ic * 128:(ic + 1) * 128],
                                                  hact[:, ic * 128:(ic + 1) * 128],
                                                  eye[:])
                            K.inc(ins, "sT")

                        @blk.scalar
                        def _(a, ct=ct):
                            K.w(a, "sT")
                            ins = None
                            for ic in range(4):
                                ins = a.copy(hactT[:, ic, ct * 128:(ct + 1) * 128],
                                             pb[2][:, ic * 128:(ic + 1) * 128])
                            K.inc(ins, "sA")

                    for ct in range(8):
                        @blk.tensor
                        def _(t, e=e, ct=ct):
                            K.w(t, "sA")
                            K.w(t, "sV")
                            ins = None
                            for ic in range(4):
                                t.matmul(pb[3][:, :],
                                         hactT[:, ic, ct * 128:(ct + 1) * 128],
                                         w2_sb[e % 2][:, ic, 0:512],
                                         start=(ic == 0), stop=(ic == 3))
                                ins = t.matmul(pb[4][:, :],
                                               hactT[:, ic, ct * 128:(ct + 1) * 128],
                                               w2_sb[e % 2][:, ic, 512:1024],
                                               start=(ic == 0), stop=(ic == 3))
                            K.inc(ins, "sT")

                        @blk.vector
                        def _(v, ct=ct):
                            K.w(v, "sT")
                            K.w(v, "qC")
                            v.tensor_copy(yst[ct % 2][:, 0:512], pb[3][:, :])
                            K.inc(v.tensor_copy(yst[ct % 2][:, 512:1024], pb[4][:, :]),
                                  "sV")

                        @blk.scalar
                        def _(a, e=e, ct=ct):
                            K.w(a, "sV")
                            K.inc(a.dma_start(
                                out=Y_own[e * 1024 + ct * 128:e * 1024 + (ct + 1) * 128, :],
                                in_=yst[ct % 2][:]), "qC")

                @blk.gpsimd
                def _(g):
                    K.w(g, "qC")
                    K.inc(g.collective_compute(
                        "AllGather", ALU.bypass, replica_groups=[list(range(8))],
                        ins=[Y_own.opt()], outs=[Y_all.opt()]), "sCC")
                    K.w(g, "sCC")
                nCC_yag = K.n("sCC")
                if sub < 4:
                    return
                s4e.close()
                phase_barrier()

                y0b = s4.enter_context(nc.sbuf_tensor("p4_y0b", [128, 1024], BF16))
                y1b = s4.enter_context(nc.sbuf_tensor("p4_y1b", [128, 1024], BF16))
                moe_t = sb4("p4_moe", [128, 1024])
                inner = sb4("p4_inner", [128, 1024])
                lnf = sb4("p4_lnf", [128, 1024])
                upd4 = [sb4(f"p4_upd{i}", [128, 1024]) for i in range(2)]

                for i in range(32):
                    @blk.vector
                    def _(v, i=i):
                        K.w(v, "qG")
                        for k in range(2):
                            v.scalar_tensor_tensor(idxf[:, k:k + 1], RT[:, i, k:k + 1],
                                                   float(1024.0), RT[:, i, 4 + k:5 + k],
                                                   ALU.mult, ALU.add)
                        v.drain()
                        v.tensor_scalar(idxf[:], idxf[:], float(65535.0), None,
                                        ALU.min, ALU.bypass)
                        v.drain()
                        K.inc(v.tensor_copy(idxu[:], idxf[:]), "sV")

                    @blk.gpsimd
                    def _(g, i=i):
                        K.w(g, "sV")
                        if i == 0:
                            K.w(g, "sCC", nCC_yag)
                        K.inc(g.indirect_dma_start(
                            out=y0b[:], out_offset=None, in_=Y_all,
                            in_offset=IndirectOffsetOnAxis(ap=idxu[:, 0:1], axis=0)),
                            "qG")
                        K.inc(g.indirect_dma_start(
                            out=y1b[:], out_offset=None, in_=Y_all,
                            in_offset=IndirectOffsetOnAxis(ap=idxu[:, 1:2], axis=0)),
                            "qG")

                    @blk.sync
                    def _(q, i=i):
                        if i >= 2:
                            K.w(q, "sV")
                        K.inc(q.dma_start(out=xt5[i % 2][:],
                                          in_=xr[i * 128:(i + 1) * 128, :]), "qS")

                    @blk.vector
                    def _(v, i=i):
                        K.w(v, "qG", K.n("qG"))
                        K.w(v, "qS", K.n("qS"))
                        v.tensor_scalar(moe_t[:], y0b[:], RT[:, i, 2:3], None,
                                        ALU.mult, ALU.bypass)
                        v.scalar_tensor_tensor(inner[:], y1b[:], RT[:, i, 3:4], moe_t[:],
                                               ALU.mult, ALU.add)
                        v.tensor_scalar(hin[:], xt5[i % 2][:], negm_s[:, i:i + 1],
                                        rstd_s[:, i:i + 1], ALU.add, ALU.mult)
                        K.inc(v.tensor_tensor(inner[:], inner[:], hin[:], ALU.add), "sV")

                    @blk.tensor
                    def _(t, i=i):
                        K.w(t, "sV")
                        ins = None
                        for ic in range(4):
                            t.matmul(pb[2][:, :], actT[:, ic, i * 128:(i + 1) * 128],
                                     swd_sb[:, ic, 0:512],
                                     start=(ic == 0), stop=(ic == 3))
                            ins = t.matmul(pb[5][:, :],
                                           actT[:, ic, i * 128:(i + 1) * 128],
                                           swd_sb[:, ic, 512:1024],
                                           start=(ic == 0), stop=(ic == 3))
                        K.inc(ins, "sT")

                    @blk.vector
                    def _(v):
                        K.w(v, "sT")
                        v.tensor_tensor(inner[:, 0:512], inner[:, 0:512], pb[2][:, :],
                                        ALU.add)
                        K.inc(v.tensor_tensor(inner[:, 512:1024], inner[:, 512:1024],
                                              pb[5][:, :], ALU.add), "sV")

                    ln_chain(inner[:, :], lnf[:, :], col=i)

                    @blk.vector
                    def _(v, i=i):
                        K.w(v, "qS")
                        K.inc(v.tensor_tensor(upd4[i % 2][:], lnf[:], xt5[i % 2][:],
                                              ALU.add), "sV")

                    @blk.sync
                    def _(q, i=i):
                        K.w(q, "sV")
                        K.inc(q.dma_start(out=out_h[i * 128:(i + 1) * 128, :],
                                          in_=upd4[i % 2][:]), "qS")

                @blk.gpsimd
                def _(g):
                    K.w(g, "qS")
                    K.w(g, "qC")
                    K.w(g, "qG")

        if stage >= 4:
            if 20 < stage < 26:
                p4(sub=0, r1sub=stage - 20)
            elif 26 <= stage < 30:
                p4(sub=2, r2sub=stage - 25)
            else:
                p4(sub=(99 if stage == 4 else stage - 30))
        global LAST_SEMS
        LAST_SEMS = {k: v[1] for k, v in K.sem.items()}

        if stage > 4:
            stage_final = -1
        @blk.vector
        def _(v):
            K.w(v, "sA")
            K.w(v, "qG")
            v.tensor_copy(dbg[:, 0:8], scal[:, :])
            K.inc(v.tensor_copy(dbg[:, 8:12], scal2[:, :]), "sV")


        @blk.gpsimd
        def _(g):
            K.w(g, "sV")
            K.inc(g.dma_start(out=dbg_h, in_=dbg[:]), "qG")

        if stage < 4 or stage > 4:
            @blk.sync
            def _(q):
                for nm in ["qC", "qV", "qT", "qG", "sT", "sV", "sA", "sG", "sCC"]:
                    K.w(q, nm)
                K.w(q, "qS")
                for j in range(4):
                    K.inc(q.dma_start(out=out_h[j * 1024:(j + 1) * 1024, :],
                                      in_=xr[j * 1024:(j + 1) * 1024, :]), "qS")

            @blk.gpsimd
            def _(g):
                K.w(g, "qS")
    return nc


def make_in_maps(inputs):
    f32 = np.float32
    x = np.ascontiguousarray(np.asarray(inputs["L0_attn_x"], f32)).reshape(8, 4096, 1024)
    w_in = np.asarray(inputs["L0_attn_w_in"], f32)
    b_in = np.asarray(inputs["L0_attn_b_in"], f32)
    w_out = np.asarray(inputs["L0_attn_w_out"], f32)
    b_out = np.asarray(inputs["L0_attn_b_out"], f32)
    wu = np.asarray(inputs["L0_ffn_w_up"], f32)
    wg = np.asarray(inputs["L0_ffn_w_gate"], f32)
    wd = np.asarray(inputs["L0_ffn_w_down"], f32)
    A = np.asarray(inputs["L1_attn_A"], f32)
    Wd = np.asarray(inputs["L1_attn_Wd"], f32)
    bd = np.asarray(inputs["L1_attn_bd"], f32)
    Wb = np.asarray(inputs["L1_attn_Wb"], f32)
    bb = np.asarray(inputs["L1_attn_bb"], f32)
    Wc = np.asarray(inputs["L1_attn_Wc"], f32)
    bc = np.asarray(inputs["L1_attn_bc"], f32)
    rmsw = np.asarray(inputs["L1_ffn_rms_w"], f32)
    gw = np.asarray(inputs["L1_ffn_gate_w"], f32)
    w13 = np.asarray(inputs["L1_ffn_w13"], f32)
    w2 = np.asarray(inputs["L1_ffn_w2"], f32)
    swg = np.asarray(inputs["L1_ffn_sw_gate"], f32)
    swu = np.asarray(inputs["L1_ffn_sw_up"], f32)
    swd = np.asarray(inputs["L1_ffn_sw_down"], f32)

    eye = np.eye(128, dtype=f32)
    eyeb = np.eye(128, dtype=BF)
    striu = np.triu(np.ones((128, 128), BF), 1)
    mask = np.triu(np.full((128, 128), -1e9, f32), 1)
    exp32 = np.zeros((32, 1024), f32)
    for b2 in range(32):
        exp32[b2, b2 * 32:(b2 + 1) * 32] = 1.0
    exp96 = np.zeros((96, 3072), f32)
    for b2 in range(96):
        exp96[b2, b2 * 32:(b2 + 1) * 32] = 1.0
    eb = np.zeros((16, 2048), BF)
    for n in range(16):
        eb[n, n * 128:(n + 1) * 128] = 1.0
    onesr = np.ones((1, 128), f32)
    onescb = np.ones((128, 1), BF)
    iota64 = np.broadcast_to(np.arange(64, dtype=f32), (128, 64)).copy()

    in_maps = []
    for c in range(8):
        h = c % 2
        mp = np.zeros((8, 1), f32)
        mp[:c] = 1.0
        in_maps.append({
            "x": x[c],
            "a_wiT": np.ascontiguousarray(w_in.T),
            "a_bin": np.ascontiguousarray(b_in.reshape(24, 128).T),
            "a_woT": np.ascontiguousarray(w_out.T),
            "a_bout": np.ascontiguousarray(b_out.reshape(8, 128).T),
            "f_wu": wu, "f_wg": wg, "f_wd": wd,
            "s_A": np.ascontiguousarray(
                A[h * 512:(h + 1) * 512].reshape(4, 128, 16).transpose(1, 0, 2)),
            "s_wdT": np.ascontiguousarray(Wd.T[:, h * 512:(h + 1) * 512]),
            "s_bd": np.ascontiguousarray(bd[h * 512:(h + 1) * 512].reshape(4, 128).T),
            "s_wbT": np.ascontiguousarray(Wb.T), "s_bb": bb.reshape(16, 1),
            "s_wcT": np.ascontiguousarray(Wc.T), "s_bc": bc.reshape(16, 1),
            "m_gwT": np.ascontiguousarray(gw.T),
            "m_w13": np.ascontiguousarray(w13[c * 8:(c + 1) * 8].reshape(8192, 1024)),
            "m_w2": np.ascontiguousarray(w2[c * 8:(c + 1) * 8].reshape(4096, 1024)),
            "m_swg": swg, "m_swu": swu, "m_swd": swd,
            "m_rmsw": np.broadcast_to(rmsw, (128, 1024)).copy(),
            "c_eye": eye, "c_eyeb": eyeb, "c_striu": striu, "c_mask": mask,
            "c_exp32": exp32, "c_exp96": exp96, "c_eb": eb,
            "c_onesr": onesr, "c_onescb": onescb, "c_iota64": iota64,
            "c_mprev": mp,
            "c_tg1": (c * 4096 + 1 + np.arange(128, dtype=f32)).reshape(128, 1),
            "c_xtbase": (h * 512 + np.arange(128, dtype=f32)).reshape(128, 1),
            "c_ytbase": (h * 4096 + np.arange(128, dtype=f32)).reshape(128, 1),
            "c_p128": np.arange(128, dtype=f32).reshape(128, 1),
        })
    return in_maps


def kernel(**inputs, ):
    in_maps = make_in_maps(inputs)
    nc = build(KSTAGE)
    res = run_bass_kernel_spmd(nc, in_maps, core_ids=list(range(8)))
    shards = [np.asarray(r["out"], np.float32) for r in res.results]
    return np.stack(shards, 0).reshape(4, 8192, 1024)


def _count(stage=4):
    import concourse.bass as _b
    nc = build(stage)
    return nc

if __name__ == "__main__":
    import sys
    st = int(sys.argv[1]) if len(sys.argv) > 1 else 4
    nc = build(st)
    print("built stage", st)
